# version 10
# baseline (speedup 1.0000x reference)
import math
import numpy as np
import ml_dtypes
import concourse.bass as bass
import concourse.mybir as mybir
from concourse.bass_utils import run_bass_kernel_spmd

F32 = mybir.dt.float32
BF16 = mybir.dt.bfloat16
AF = mybir.ActivationFunctionType
ALU = mybir.AluOpType
RND_C = 1.5 * 2 ** 23
ENGINES = ("pe", "act", "dve", "sp", "gp")
BAND_L = 384


class Builder:
    NRING = 8
    FOLD_LIM = 7

    def __init__(self):
        self.q = {e: [] for e in ENGINES}
        self.val = {}
        self.ring_idx = {"sp": 0, "gp": 0}
        self.relay_fn = None
        self.relay_fns = None
        self._relay_ring = [None] * 8
        self._relay_i = 0
        self._in_relay = False

    def _relay(self, ws):
        k = self._relay_i % 8
        self._relay_i += 1
        fn = self.relay_fns[k] if self.relay_fns else self.relay_fn
        self._in_relay = True
        rtok = self.emit("dve", list(ws) + [self._relay_ring[k]], fn)
        self._in_relay = False
        self._relay_ring[k] = rtok
        return rtok

    def _bump(self, sem, inc):
        self.val[sem] = self.val.get(sem, 0) + inc
        return (sem, self.val[sem])

    def emit(self, eng, waits, fn, inc=1):
        coll = {}
        for w in waits:
            if w is not None and coll.get(w[0], -1) < w[1]:
                coll[w[0]] = w[1]
        waits = [(s, v) for s, v in coll.items()]
        if eng == "act" and self.relay_fn is not None:
            pe_waits = [w for w in waits if w[0] == "pe"]
            if pe_waits:
                waits = [w for w in waits if w[0] != "pe"]
                waits.append(self._relay(pe_waits))
        if self.relay_fn is not None and not self._in_relay:
            while len(waits) > self.FOLD_LIM:
                head, waits = waits[:self.FOLD_LIM], waits[self.FOLD_LIM:]
                waits = waits + [self._relay(head)]
        if eng in ("sp", "gp"):
            r = self.ring_idx[eng]
            self.ring_idx[eng] = (r + 1) % self.NRING
            sem = f"{eng}{r}"
            prev = self.val.get(sem, 0)
            if prev:
                waits.append((sem, prev))
            tok = self._bump(sem, 16)
            self.q[eng].append((waits, fn, 16, sem))
            return tok
        sem = eng
        tok = self._bump(sem, inc)
        q = "gp" if eng == "gpc" else eng
        self.q[q].append((waits, fn, inc, sem))
        return tok

    def dma_totals(self, eng):
        return [(f"{eng}{r}", self.val.get(f"{eng}{r}", 0))
                for r in range(self.NRING) if self.val.get(f"{eng}{r}", 0)]

    def build(self, block, sems):
        def mk(eng_name):
            def body(engine):
                waited = {}
                for waits, fn, inc, sem_name in self.q[eng_name]:
                    for (sname, sval) in waits:
                        if waited.get(sname, -1) < sval:
                            engine.wait_ge(sems[sname], sval)
                            waited[sname] = sval
                    inst = fn(engine)
                    if inc > 0:
                        assert inst is not None, f"{eng_name} step returned None"
                        inst.then_inc(sems[sem_name], inc)
            return body
        block.tensor(mk("pe"))
        block.scalar(mk("act"))
        block.vector(mk("dve"))
        block.sync(mk("sp"))
        block.gpsimd(mk("gp"))


class Prog:
    def __init__(self, S=2048, D=1024, H=16, DH=4096, debug=False):
        self.debug = debug
        self.S, self.D, self.H, self.DH, self.DK = S, D, H, DH, 64
        self.NT, self.KC, self.HC, self.NP = S // 128, D // 128, DH // 128, H // 2
        self.NB = S // 512
        self.RW = 128
        self.NREG = S // self.RW
        assert self.NP == self.KC
        self.inv8 = 0.125
        self._tr_tok = None
        self._s2s_tok = None
        self.nc = bass.Bass("TRN2", target_bir_lowering=False)
        self.b = Builder()
        self._declare_dram()
        self._alloc_sbuf()

    def _declare_dram(self):
        nc, S, D, DH = self.nc, self.S, self.D, self.DH
        dt = nc.dram_tensor
        self.x_in = dt("x_in", [S, D], F32, kind="ExternalInput")
        self.wq_in = dt("wq_in", [D, D], F32, kind="ExternalInput")
        self.wk_in = dt("wk_in", [D, D], F32, kind="ExternalInput")
        self.wv_in = dt("wv_in", [D, D], F32, kind="ExternalInput")
        self.wo_in = dt("wo_in", [D, D], F32, kind="ExternalInput")
        self.bo_in = dt("bo_in", [1, D], F32, kind="ExternalInput")
        self.rk_in = dt("rk_in", [65, 64], F32, kind="ExternalInput")
        self.rv_in = dt("rv_in", [65, 64], F32, kind="ExternalInput")
        self.w1_in = dt("w1_in", [DH, D], F32, kind="ExternalInput")
        self.cw_in = dt("cw_in", [DH, 3], F32, kind="ExternalInput")
        self.cb_in = dt("cb_in", [1, DH], F32, kind="ExternalInput")
        self.w2_in = dt("w2_in", [D, DH], F32, kind="ExternalInput")
        self.out_d = dt("out", [S, D], F32, kind="ExternalOutput")
        self.stg_q = dt("stg_q", [D, D], BF16)
        self.stg_k = dt("stg_k", [D, D], BF16)
        self.stg_v = dt("stg_v", [D, D], BF16)
        self.stg_o = dt("stg_o", [D, D], BF16)
        self.stg_w1 = dt("stg_w1", [DH, D], BF16)
        self.stg_w2 = dt("stg_w2", [D, DH], BF16)
        self.stg_w2t = dt("stg_w2t", [DH, D], BF16)
        self.stg_w1t = dt("stg_w1t", [D, DH], BF16)
        self.stg_qt = dt("stg_qt", [D, D], BF16)
        self.stg_kt = dt("stg_kt", [D, D], BF16)
        self.stg_vt = dt("stg_vt", [D, D], BF16)
        self.ctx_d = dt("ctx_d", [D, S], BF16)
        self.ctxn_d = dt("ctxn_d", [D, S], BF16)
        self.x_d = dt("x_d", [128, (D // 128) * S], F32)
        self.sums_d = dt("sums_d", [3 * self.H, S], F32)
        self.band_f = dt("band_f", [1, 8 * 128 * BAND_L], BF16)
        self.auband_d = dt("auband_d", [1, 128 + 4 * 128 * 4 * BAND_L], BF16)
        self.dbg_d = {}

    def tap(self, name, ap, waits):
        if not self.debug:
            return None
        shape = [ap.shape[0], int(np.prod(ap.shape[1:]))]
        t = self.nc.dram_tensor("dbg_" + name, shape, ap.dtype,
                                kind="ExternalOutput")
        self.dbg_d[name] = shape
        return self.dma("sp", t[:, :], ap, waits)

    def _alloc_sbuf(self):
        nc = self.nc
        S, D, DH, KC, NT, HC, NP, H = (self.S, self.D, self.DH, self.KC, self.NT,
                                       self.HC, self.NP, self.H)
        ARENA = 200 * 1024
        arena = nc.alloc_sbuf_tensor("arena", [128, ARENA], mybir.dt.uint8)
        base = nc.lookup_mloc(arena).addr
        self._off = base
        self._base = base
        self._offs = {}

        def at(name, shape, dtype, alias=None):
            nbytes = int(np.prod(shape[1:])) * mybir.dt.size(dtype)
            nbytes = (nbytes + 31) & ~31
            off = self._off if alias is None else alias
            if alias is None:
                self._off += nbytes
            self._offs[name] = (off, off + nbytes)
            t = nc.alloc_sbuf_tensor_at(name, shape, dtype, offset=off)
            return t

        self.xbf = at("xbf", [128, KC * S], BF16)
        self.q_pair = at("q_pair", [128, S], BF16)
        self.khi = at("khi", [128, S], BF16)
        self.klo = at("klo", [128, S], BF16)
        self.VW = 2 * 2 * 67
        self.v_tm = at("v_tm", [128, NT * self.VW], BF16)
        self.au = at("au", [128, 2 * S], BF16)
        self.wqp = at("wqp", [128, 2 * KC * 128], BF16)
        self.wkp = at("wkp", [128, 2 * KC * 128], BF16)
        self.wvp = at("wvp", [128, 2 * KC * 128], BF16)
        self.fband = at("fband", [128, 4 * BAND_L], BF16)
        self.wskew = at("wskew", [128, 4 * BAND_L], BF16)
        self.au_band = at("au_band", [128, 2 * 4 * BAND_L], BF16)
        self.abT = at("abT", [128, 4 * BAND_L], BF16)
        self.bb = at("bb", [128, 2 * 4 * 128], BF16)
        self.battn = at("battn", [127, 4 * 128], BF16)
        attn_pool_start = self._offs["q_pair"][0]
        attn_pool_end = self._off
        self.wstg = at("wstg", [128, 2 * 2048], F32)
        self.wbf16 = at("wbf16", [128, 2 * 2048], BF16)
        _need = max(KC * S * 4 + KC * D * 2 + 2048,
                    KC * DH * 2 + HC * 128 * 4 + 8 * 130 * 4)
        _nat = self._off - attn_pool_start
        self.poolpad = at("poolpad", [128, max(32, _need - _nat)],
                          mybir.dt.uint8)
        wstg_start = self._offs["wstg"][0]
        self.rv4 = at("rv4", [4, 128], F32)
        self.sel2 = at("sel2", [2, 128], F32)
        self.ident = at("ident", [128, 128], BF16)
        self.identf = at("identf", [128, 128], F32)
        self.ones_col_f = at("ones_col_f", [128, 1], F32)
        self.ones_r1f = at("ones_r1f", [1, 128], F32)
        self.onespadB = at("onespadB", [1, 128], F32)
        self.onespadA = at("onespadA", [1, 128], F32)
        self.zeros_b = at("zeros_b", [1, 512], BF16)
        self.zcol67 = at("zcol67", [1, 67], BF16)
        self.zcol128 = at("zcol128", [1, 128], BF16)
        self.mask_hi = at("mask_hi", [128, 128], BF16)
        self.mask_lo = at("mask_lo", [128, 128], BF16)
        self.mask_bb0 = at("mask_bb0", [128, 127], BF16)
        self.mask_bbL = at("mask_bbL", [128, 127], BF16)
        self.rkq = at("rkq", [128, 65], BF16)
        self.rvb = at("rvb", [127, 64], BF16)
        self.rv64row = at("rv64row", [1, 64], F32)
        self.onespadC = at("onespadC", [1, 127], F32)
        self.rk64c = at("rk64c", [128, 1], F32)
        self.rk0c = at("rk0c", [128, 1], F32)
        self.boc = at("boc", [128, KC], F32)
        self.cwc = at("cwc", [128, 3 * HC], F32)
        self.cbc = at("cbc", [128, HC], F32)
        self.swc = at("swc", [128, 4], F32)
        self.rndc_col = at("rndc_col", [128, 1], F32)
        self.eps_col = at("eps_col", [128, 1], F32)
        self.stat = at("stat", [128, 64], F32)
        self.shift32 = at("shift32", [32, 128], F32)
        self.x_tm = at("x_tm", [128, max(D, 1024)], F32)
        self.ctmp = at("ctmp", [64, S], BF16)
        self.sums3 = at("sums3", [67, S], F32)
        self.tmp_f = at("tmp_f", [128, S], F32)
        self.tmp_f2 = at("tmp_f2", [128, S], F32)
        self.wabs = at("wabs", [128, 64], F32)
        self.relt = at("relt", [65, 64], F32)
        self.rel2 = at("rel2", [64, 68], F32)
        peak_static = self._off
        pool = attn_pool_start
        pool_end = self._offs["poolpad"][1]
        blk2 = self._offs["sums3"][0]
        blk2_end = self._offs["tmp_f2"][1]
        self.x_f = at("x_f", [128, KC * S], F32, alias=pool)
        self.woT = at("woT", [128, KC * D], BF16, alias=pool + KC * S * 4)
        self.ctxc = at("ctxc", [128, 2 * 512], BF16,
                       alias=pool + KC * S * 4 + KC * D * 2)
        assert pool + KC * S * 4 + KC * D * 2 + 2048 <= pool_end
        self.nstat = at("nstat", [4, S], F32, alias=blk2)
        self.rstat_raw = at("rstat_raw", [2, S], F32, alias=blk2 + S * 4)
        self.rstat = at("rstat", [2, S], F32, alias=blk2 + 2 * S * 4)
        assert blk2 + 3 * S * 4 <= blk2_end
        self.rowB = at("rowB", [1, S], F32, alias=blk2)
        self.srow = at("srow", [1, S], F32, alias=blk2 + S * 4)
        self.arow = at("arow", [1, S], F32, alias=blk2 + 2 * S * 4)
        self.w1T = at("w1T", [128, KC * DH], BF16, alias=pool)
        nxt = pool + KC * DH * 2
        self.h2reg = at("h2reg", [128, HC * 128], F32, alias=nxt)
        nxt += HC * 128 * 4
        self.cbuf = at("cbuf", [128, 8 * 130], F32, alias=nxt)
        nxt += 8 * 130 * 4
        assert nxt <= pool_end, (nxt, pool_end)
        self.w2Tc = at("w2Tc", [128, 2 * D], BF16, alias=blk2)
        assert 2 * D * 2 <= S * 4
        if 2 * D * 2 + 2 * 128 * 2 + 128 * 4 <= S * 4:
            qb = blk2 + 2 * D * 2
        else:
            qb = self._offs["ctmp"][0]
            assert 2 * 128 * 2 + 128 * 4 <= S * 2
        self.xq2c = at("xq2c", [128, 2 * 128], BF16, alias=qb)
        self.a2sb = at("a2sb", [128, 128], F32, alias=qb + 2 * 128 * 2)
        total_arena = peak_static - self._base
        assert total_arena <= 200 * 1024, total_arena
        self.psA = nc.alloc_psum_tensor("psA", [128, 2048], F32)
        self.psB = nc.alloc_psum_tensor("psB", [128, 2048], F32)
        self.relay_buf = at("relay_buf", [1, 64], F32)
        self.b.relay_fn = lambda e: e.memset(self.relay_buf[0:1, 0:2], 0.0)
        self.b.relay_fns = [
            (lambda e, o=8 * k: e.memset(self.relay_buf[0:1, o:o + 2], 0.0))
            for k in range(8)]

    def dma(self, eng, out, in_, waits, slow=False, s2s=False):
        if s2s:
            waits = list(waits) + [self._tr_tok]
        if slow:
            t = self.b.emit(eng, waits, lambda e: e.dma_start(
                out=out, in_=in_, allow_slow_non_contiguous=True), inc=16)
        else:
            t = self.b.emit(eng, waits, lambda e: e.dma_start(out=out, in_=in_),
                            inc=16)
        if s2s:
            self._s2s_tok = t
        return t

    def dmaT(self, out, in_, waits):
        t = self.b.emit("sp", list(waits) + [self._s2s_tok, self._tr_tok],
                        lambda e: e.dma_start(out=out, in_=in_, transpose=True),
                        inc=16)
        self._tr_tok = t
        return t

    def mm(self, out, lhsT, rhs, waits, start, stop):
        return self.b.emit("pe", waits,
                           lambda e: e.matmul(out, lhsT, rhs, start=start, stop=stop),
                           inc=1)

    def tr(self, out, in_, waits, start=True, stop=True, f32=False):
        idn = self.identf if f32 else self.ident
        P = in_.shape[0]
        return self.b.emit("pe", waits,
                           lambda e: e.matmul(out, in_, idn[0:P, 0:P],
                                              start=start, stop=stop,
                                              is_transpose=True),
                           inc=1)

    def consts(self):
        def msets(e):
            e.memset(self.ident[:, :], 1.0)
            e.memset(self.identf[:, :], 1.0)
            e.memset(self.ones_col_f[:, :], 1.0)
            e.memset(self.ones_r1f[:, :], 1.0)
            e.memset(self.onespadB[:, 0:64], 0.0)
            e.memset(self.onespadB[:, 64:128], 1.0)
            e.memset(self.onespadA[:, 0:64], 1.0)
            e.memset(self.onespadA[:, 64:128], 0.0)
            e.memset(self.onespadC[:, 0:63], 0.0)
            e.memset(self.onespadC[:, 63:127], 1.0)
            e.memset(self.zeros_b[:, :], 0.0)
            e.memset(self.zcol67[:, :], 0.0)
            e.memset(self.zcol128[:, :], 0.0)
            e.memset(self.mask_hi[:, :], 1.0)
            e.memset(self.mask_lo[:, :], 1.0)
            e.memset(self.mask_bb0[:, :], 1.0)
            e.memset(self.mask_bbL[:, :], 1.0)
            e.memset(self.wskew[:, :], 0.0)
            e.memset(self.au_band[:, :], 0.0)
            e.memset(self.bb[:, :], 0.0)
            e.memset(self.rv4[:, :], 0.0)
            e.memset(self.sel2[:, :], 1.0)
            e.memset(self.stat[:, :], 0.0)
            e.memset(self.wabs[:, 32:48], 127.0)
            e.memset(self.rndc_col[:, :], RND_C)
            e.memset(self.eps_col[:, :], 1e-5)
            for q in range(4):
                o = q * BAND_L
                e.memset(self.fband[:, o:o + 97], 0.0)
                e.memset(self.fband[:, o + 160:o + 224], 1.0)
                e.memset(self.fband[:, o + 224:o + BAND_L], 0.0)
            vt5 = self.v_tm[:, :].rearrange("p (t h v c) -> p t h v c", h=2, v=2, c=67)
            e.memset(vt5[:, :, :, 0:1, 64:66], 1.0)
            e.memset(vt5[:, :, :, 0:1, 66:67], 0.0)
            e.memset(vt5[:, :, :, 1:2, 64:65], 1.0)
            e.memset(vt5[:, :, :, 1:2, 65:66], 0.0)
            return e.memset(vt5[:, :, :, 1:2, 66:67], 1.0)
        t_m = self.b.emit("dve", [], msets)

        def aff(e):
            e.affine_select(self.ident[:, :], self.ident[:, :], [[-1, 128]],
                            ALU.is_equal, 0.0, base=0, channel_multiplier=1)
            e.affine_select(self.identf[:, :], self.identf[:, :], [[-1, 128]],
                            ALU.is_equal, 0.0, base=0, channel_multiplier=1)
            e.affine_select(self.mask_hi[:, :], self.mask_hi[:, :], [[-1, 128]],
                            ALU.is_ge, 0.0, base=0, channel_multiplier=1)
            e.affine_select(self.mask_lo[:, :], self.mask_lo[:, :], [[1, 128]],
                            ALU.is_ge, 0.0, base=0, channel_multiplier=-1)
            e.affine_select(self.mask_bb0[:, :], self.mask_bb0[:, :], [[1, 127]],
                            ALU.is_ge, 0.0, base=-31, channel_multiplier=1)
            e.affine_select(self.sel2[:, :], self.sel2[:, :], [[1, 128]],
                            ALU.is_ge, 0.0, base=0, channel_multiplier=-64)
            return e.affine_select(self.mask_bbL[:, :], self.mask_bbL[:, :],
                                   [[-1, 127]], ALU.is_ge, 0.0, base=158,
                                   channel_multiplier=-1)
        t_a0 = self.b.emit("gpc", [t_m], aff)
        self.t_const = self.b.emit("gpc", [t_a0], lambda e: e.affine_select(
            self.sel2[:, :], self.sel2[:, :], [[-1, 128]],
            ALU.is_ge, 0.0, base=63, channel_multiplier=64))

    def small_inputs(self):
        b = self.b
        t1 = self.dma("sp", self.relt[:, :], self.rk_in[:, :], [self.t_const])
        tt = self.tr(self.psA[0:64, 0:65], self.relt[:, 0:64], [t1], f32=True)
        tev = b.emit("dve", [tt], lambda e: e.tensor_copy(self.rel2[:, 0:65],
                                                          self.psA[0:64, 0:65]))
        trkq = b.emit("dve", [tev], lambda e: e.tensor_scalar(
            self.rkq[0:64, :], self.rel2[:, 0:65], self.rel2[:, 64:65], None,
            ALU.subtract))
        t64 = b.emit("dve", [tev], lambda e: e.tensor_copy(self.rk64c[0:64, :],
                                                           self.rel2[:, 64:65]))
        t0 = b.emit("dve", [tev], lambda e: e.tensor_copy(self.rk0c[0:64, :],
                                                          self.rel2[:, 0:1]))
        tds = [self.dma("gp", self.rkq[64:128, :], self.rkq[0:64, :], [trkq], s2s=True),
               self.dma("gp", self.rk64c[64:128, :], self.rk64c[0:64, :], [t64], s2s=True),
               self.dma("gp", self.rk0c[64:128, :], self.rk0c[0:64, :], [t0], s2s=True)]
        t64r = self.dma("sp", self.rv64row[:, :], self.rv_in[64:65, 0:64],
                        [self.t_const])
        tmm64 = self.mm(self.psA[0:127, 0:64], self.onespadC[:, :],
                        self.rv64row[:, :], [t64r, tev], start=True, stop=True)
        tev64 = b.emit("dve", [tmm64], lambda e: e.tensor_copy(
            self.rvb[0:127, :], self.psA[0:127, 0:64]))
        trvb = self.dma("gp", self.rvb[0:63, :], self.rv_in[1:64, 0:64],
                        [tev64])
        tr4 = [self.dma("gp", self.rv4[0:1, 0:64], self.rv_in[64:65, 0:64],
                        [self.t_const]),
               self.dma("gp", self.rv4[1:2, 0:64], self.rv_in[0:1, 0:64],
                        [self.t_const]),
               self.dma("gp", self.rv4[2:3, 64:128], self.rv_in[64:65, 0:64],
                        [self.t_const]),
               self.dma("gp", self.rv4[3:4, 64:128], self.rv_in[0:1, 0:64],
                        [self.t_const])]
        tb = [self.dma("sp", self.boc[:, k:k + 1],
                       bass.AP(self.bo_in, k * 128, [[1, 128], [128, 1]]),
                       [self.t_const], slow=True) for k in range(self.KC)]
        tcw = [self.dma("sp", self.cwc[:, 3 * h:3 * h + 3],
                        bass.AP(self.cw_in, h * 384, [[3, 128], [1, 3]]),
                        [self.t_const]) for h in range(self.HC)]
        tcb = [self.dma("sp", self.cbc[:, h:h + 1],
                        bass.AP(self.cb_in, h * 128, [[1, 128], [128, 1]]),
                        [self.t_const], slow=True) for h in range(self.HC)]
        tpad = self.dma("gp", bass.AP(self.auband_d, 0, [[1, 128]]),
                        self.zeros_b[0:1, 0:128], [self.t_const])
        self.t_small = tds + [trvb, tpad] + tb + tcw + tcb + tr4
        self._psA_init = tev64

    def load_x(self, to_xf=False, dep=()):
        b, S, KC, NT = self.b, self.S, self.KC, self.NT
        prev = [self.t_const, self._psA_init] + list(dep)
        for tt in range(NT):
            tl = self.dma("sp", self.x_tm[:, 0:self.D],
                          self.x_in[tt * 128:(tt + 1) * 128, :], prev)
            evs = []
            for kc in range(KC):
                ps = self.psA if kc < 4 else self.psB
                bank = ps[:, (kc % 4) * 512:(kc % 4) * 512 + 128]
                ttr = self.tr(bank, self.x_tm[:, kc * 128:(kc + 1) * 128],
                              [tl] + prev, start=True, stop=True, f32=True)
                if to_xf:
                    dst = self.x_f[:, kc * S + tt * 128: kc * S + tt * 128 + 128]
                    evs.append(b.emit("dve", [ttr],
                                      lambda e, d=dst, s=bank: e.tensor_copy(d, s)))
                else:
                    dst = self.xbf[:, kc * S + tt * 128: kc * S + tt * 128 + 128]
                    evs.append(b.emit("dve", [ttr],
                                      lambda e, d=dst, s=bank: e.tensor_copy(d, s)))
            prev = evs[-2:]
        if to_xf:
            self.t_xf = prev
        else:
            self.t_x = prev

    def weights_qkvo(self):
        b, D, KC = self.b, self.D, self.KC
        self.o_pend = None
        self.t_wT = {}
        prev = [None, None]
        for nm, win, stg, stgT in (("q", self.wq_in, self.stg_q, self.stg_qt),
                                   ("k", self.wk_in, self.stg_k, self.stg_kt),
                                   ("v", self.wv_in, self.stg_v, self.stg_vt),
                                   ("o", self.wo_in, self.stg_o, None)):
            tst = []
            per = min(2048, D)
            ngrp = (D * D) // (128 * per)
            for gi in range(ngrp):
                pp = gi % 2
                src = win[:, :].rearrange("r c -> (r c)")[
                    gi * 128 * per:(gi + 1) * 128 * per].rearrange(
                    "(p n) -> p n", p=128)
                tl = self.dma("sp", self.wstg[:, pp * 2048: pp * 2048 + per], src,
                              [self.t_const, prev[pp]])
                tc = b.emit("dve", [tl], lambda e, pp=pp, per=per: e.tensor_copy(
                    self.wbf16[:, pp * 2048: pp * 2048 + per],
                    self.wstg[:, pp * 2048: pp * 2048 + per]))
                dst = stg[:, :].rearrange("r c -> (r c)")[
                    gi * 128 * per:(gi + 1) * 128 * per].rearrange(
                    "(p n) -> p n", p=128)
                ts_ = self.dma("sp", dst,
                               self.wbf16[:, pp * 2048: pp * 2048 + per], [tc])
                prev[pp] = ts_
                tst.append(ts_)
            if nm == "o":
                self.o_pend = tst
                continue
            toks = []
            prev_wr = [prev[0], prev[1]]
            NP = self.NP
            for kc in range(KC):
                pp = kc % 2
                tk = self.dmaT(self.wbf16[:, pp * 2048: pp * 2048 + D],
                               stg[0:D, kc * 128:(kc + 1) * 128],
                               tst + [prev_wr[pp]])
                tw = self.dma("sp",
                              bass.AP(stgT, kc * 128,
                                      [[KC * 128, 128], [128 * D, NP], [1, 128]]),
                              self.wbf16[:, pp * 2048: pp * 2048 + D].rearrange(
                                  "f (p c) -> f p c", c=128),
                              [tk])
                prev_wr[pp] = tw
                toks.append(tw)
            self.t_wT[nm] = toks
            for pp in range(2):
                if prev_wr[pp] is not prev[pp]:
                    prev[pp] = prev_wr[pp]
        self._wstg_war = list(prev)

    def weights_w12(self):
        b, D, DH, KC, HC = self.b, self.D, self.DH, self.KC, self.HC

        def groups(rows, cols):
            total = rows * cols
            per_n = min(2048, total // 128)
            return total // (128 * per_n), per_n

        toks_abs = []
        prev = list(self._wstg_war)
        g = 0
        self.ng1 = groups(DH, D)[0]
        for win, rows, cols in ((self.w1_in, DH, D), (self.w2_in, D, DH)):
            ngrp, per_n = groups(rows, cols)
            for gi in range(ngrp):
                pp = gi % 2
                src = win[:, :].rearrange("r c -> (r c)")[
                    gi * 128 * per_n:(gi + 1) * 128 * per_n].rearrange(
                    "(p n) -> p n", p=128)
                tl = self.dma("sp", self.wstg[:, pp * 2048: pp * 2048 + per_n], src,
                              [self.t_const, prev[pp]])
                ta = b.emit("dve", [tl], lambda e, pp=pp, g=g, per_n=per_n:
                            e.reduce_sum(self.wabs[:, g:g + 1],
                                         self.wstg[:, pp * 2048: pp * 2048 + per_n],
                                         axis=mybir.AxisListType.X,
                                         apply_absolute_value=True))
                prev[pp] = ta
                toks_abs.append(ta)
                g += 1
        ngtot = g
        tmm = self.mm(self.psA[0:1, 0:ngtot], self.ones_col_f[:, :],
                      self.wabs[:, 0:ngtot], toks_abs + self.t_x,
                      start=True, stop=True)

        st = self.stat[0:1, :]
        ts1 = b.emit("dve", [tmm], lambda e: (
            e.reduce_sum(st[:, 0:1], self.psA[0:1, 0:self.ng1],
                         axis=mybir.AxisListType.X),
            e.reduce_sum(st[:, 1:2], self.psA[0:1, self.ng1:ngtot],
                         axis=mybir.AxisListType.X))[-1])
        ts2 = b.emit("dve", [ts1], lambda e: (
            e.tensor_scalar(st[:, 2:3], st[:, 0:1],
                            1.0 / (DH * D), 1e-5, ALU.mult, ALU.max),
            e.tensor_scalar(st[:, 3:4], st[:, 1:2],
                            1.0 / (DH * D), 1e-5, ALU.mult, ALU.max))[-1])
        tdv = b.emit("dve", [ts2], lambda e: e.reciprocal(
            st[:, 4:6], st[:, 2:4]))
        tsc = b.emit("dve", [tdv], lambda e: e.tensor_scalar(
            st[:, 6:8], st[:, 2:4], 1.0 / 127.0, None, ALU.mult))

        def reorder(e):
            e.tensor_copy(st[:, 8:9], st[:, 4:5])
            e.tensor_copy(st[:, 9:10], st[:, 6:7])
            e.tensor_copy(st[:, 10:11], st[:, 5:6])
            return e.tensor_copy(st[:, 11:12], st[:, 7:8])
        tro = b.emit("dve", [tsc], reorder)
        tbc = self.mm(self.psB[0:128, 0:4], self.ones_r1f[:, :], st[:, 8:12],
                      [tro], start=True, stop=True)
        tswc = b.emit("dve", [tbc], lambda e: e.tensor_copy(self.swc[:, :],
                                                            self.psB[0:128, 0:4]))
        self.t_swc = tswc
        prev = [None, None]
        tq_out = []
        for wi, (win, stg, rows, cols, scol) in enumerate(
                ((self.w1_in, self.stg_w1, DH, D, 0),
                 (self.w2_in, self.stg_w2, D, DH, 2))):
            ngrp, per_n = groups(rows, cols)
            for gi in range(ngrp):
                pp = gi % 2
                sl = slice(pp * 2048, pp * 2048 + per_n)
                src = win[:, :].rearrange("r c -> (r c)")[
                    gi * 128 * per_n:(gi + 1) * 128 * per_n].rearrange(
                    "(p n) -> p n", p=128)
                tl = self.dma("sp", self.wstg[:, sl], src, [tswc, prev[pp]])
                ta0 = b.emit("dve", [tl, tswc], lambda e, sl=sl, scol=scol:
                             e.tensor_scalar(self.wstg[:, sl], self.wstg[:, sl],
                                             self.swc[:, scol:scol + 1], None,
                                             ALU.mult))
                ta = b.emit("dve", [ta0], lambda e, sl=sl: e.tensor_scalar(
                    self.wstg[:, sl], self.wstg[:, sl], RND_C, None, ALU.add))
                t1 = b.emit("dve", [ta], lambda e, sl=sl: e.tensor_scalar(
                    self.wstg[:, sl], self.wstg[:, sl], RND_C, -1.0,
                    ALU.subtract, ALU.max))
                t2 = b.emit("dve", [t1], lambda e, sl=sl: e.tensor_scalar(
                    self.wbf16[:, sl], self.wstg[:, sl], 1.0, None, ALU.min))
                dst = stg[:, :].rearrange("r c -> (r c)")[
                    gi * 128 * per_n:(gi + 1) * 128 * per_n].rearrange(
                    "(p n) -> p n", p=128)
                ts_ = self.dma("sp", dst, self.wbf16[:, sl], [t2])
                prev[pp] = ts_
                tq_out.append(ts_)
        self.t_wq_done = tq_out
        self.t_w2t = []
        prev_wr = None
        for hc in range(HC):
            pp = hc % 2
            tk = self.dmaT(self.wbf16[:, pp * 2048: pp * 2048 + D],
                           self.stg_w2[0:D, hc * 128:(hc + 1) * 128],
                           tq_out + [prev_wr])
            tw = self.dma("sp", self.stg_w2t[hc * 128:(hc + 1) * 128, :],
                          self.wbf16[:, pp * 2048: pp * 2048 + D], [tk])
            prev_wr = tw
            self.t_w2t.append(tw)
        self.t_w1t = []
        chunk = min(2048, DH)
        nch = DH // chunk
        prev_wr = [prev_wr, prev_wr]
        ci = 0
        for kc in range(KC):
            for h in range(nch):
                pp = ci % 2
                ci += 1
                tk = self.dmaT(self.wbf16[:, pp * 2048: pp * 2048 + chunk],
                               self.stg_w1[h * chunk:(h + 1) * chunk,
                                           kc * 128:(kc + 1) * 128],
                               tq_out + [prev_wr[pp]])
                tw = self.dma("sp",
                              self.stg_w1t[kc * 128:(kc + 1) * 128,
                                           h * chunk:(h + 1) * chunk],
                              self.wbf16[:, pp * 2048: pp * 2048 + chunk], [tk])
                prev_wr[pp] = tw
                self.t_w1t.append(tw)

    def _proj_pair(self, p):
        b, S, D, KC, NT = self.b, self.S, self.D, self.KC, self.NT
        pg = p % 2
        wsl = slice(pg * KC * 128, pg * KC * 128 + KC * 128)
        tw = []
        for buf, stgT in ((self.wqp, self.stg_qt), (self.wkp, self.stg_kt),
                          (self.wvp, self.stg_vt)):
            src = bass.AP(stgT, p * 128 * D, [[KC * 128, 128], [1, KC * 128]])
            tw.append(self.dma("sp", buf[:, wsl], src,
                      self.t_wT["q" if buf is self.wqp else
                                ("k" if buf is self.wkp else "v")] +
                      [self._wslice_war[pg]]))
        tq, tk, tv = [], [], []
        bank_war = {}
        for which, wT in (("q", self.wqp), ("k", self.wkp)):
            for nb in range(self.NB):
                bank = self.psA[:, nb * 512:(nb + 1) * 512]
                last = None
                for kc in range(KC):
                    last = self.mm(bank,
                                   wT[:, pg * KC * 128 + kc * 128:
                                      pg * KC * 128 + kc * 128 + 128],
                                   self.xbf[:, kc * S + nb * 512: kc * S + nb * 512 + 512],
                                   tw + self.t_x + [self._pair_war, self._psA_war,
                                                    bank_war.get(nb)] +
                                   self._dbg_pair_wars,
                                   start=(kc == 0), stop=(kc == KC - 1))
                if which == "q":
                    te = b.emit("dve", [last], lambda e, nb=nb, bank=bank:
                                e.tensor_copy(self.q_pair[:, nb * 512:(nb + 1) * 512],
                                              bank))
                    tq.append(te)
                    bank_war[nb] = te
                else:
                    te1 = b.emit("dve", [last] + self.t_small,
                                 lambda e, nb=nb, bank=bank: e.tensor_scalar(
                                     self.khi[:, nb * 512:(nb + 1) * 512], bank,
                                     self.rk64c[:, :], None, ALU.add))
                    te2 = b.emit("dve", [te1], lambda e, nb=nb, bank=bank:
                                 e.tensor_scalar(self.klo[:, nb * 512:(nb + 1) * 512],
                                                 bank, self.rk0c[:, :], None, ALU.add))
                    tk += [te1, te2]
                    bank_war[nb] = te2
        self._psA_war = (tq + tk)[-1]
        for tt in range(NT):
            slot = self.psB[:, (tt % 4) * 512:(tt % 4) * 512 + 128]
            last = None
            for kc in range(KC):
                last = self.mm(slot,
                               self.xbf[:, kc * S + tt * 128: kc * S + tt * 128 + 128],
                               self.wvp[:, pg * KC * 128 + kc * 128:
                                         pg * KC * 128 + kc * 128 + 128],
                               tw + self.t_x +
                               [self._pair_war, self._vwar[tt % 4], self.t_swc] +
                               self._dbg_pair_wars,
                               start=(kc == 0), stop=(kc == KC - 1))
            evs = []
            for hp in range(2):
                for var in range(2):
                    base = tt * self.VW + hp * 134 + var * 67
                    evs.append(b.emit("dve", [last] + [self.t_const],
                                      lambda e, base=base, hp=hp, slot=slot:
                                      e.tensor_copy(self.v_tm[:, base:base + 64],
                                                    slot[:, hp * 64:hp * 64 + 64])))
            self._vwar[tt % 4] = evs[-1]
            tv += evs
        self._wslice_war[pg] = (tq + tk + tv)[-1]
        return tq, tk, tv

    def _band_head(self, p, hp, dep, t_zero):
        b, S, NT = self.b, self.S, self.NT
        hs = slice(hp * 64, hp * 64 + 64)
        toks_mm = []
        for qtg in range((NT + 3) // 4):
            qts = [qtg * 4 + i for i in range(4) if qtg * 4 + i < NT]
            nqt = len(qts)
            pm = [self.mm(self.psA[:, i * 512: i * 512 + 65],
                          self.q_pair[hs, qt * 128: qt * 128 + 128],
                          self.rkq[hs, 0:65],
                          dep + [self._psA_war], start=True, stop=True)
                  for i, qt in enumerate(qts)]
            psrc = self.psA[:, 0:nqt * 512].rearrange(
                "x (q c) -> x q c", c=512)[:, :, 1:64]
            fdst = self.fband[:, 0:nqt * BAND_L].rearrange(
                "x (q c) -> x q c", c=BAND_L)[:, :, 97:160]
            texp = b.emit("act", pm + [self._fband_war],
                          lambda e, psrc=psrc, fdst=fdst: e.activation(
                              fdst, psrc, AF.Exp, scale=self.inv8))
            self._psA_war = texp
            slot0 = (hp * 4) * 128 * BAND_L
            wdst = bass.AP(self.band_f, slot0,
                           [[BAND_L, 128], [128 * BAND_L, nqt], [1, BAND_L]])
            wsrc = bass.AP(self.fband, 0,
                           [[4 * BAND_L, 128], [BAND_L, nqt], [1, BAND_L]])
            twr = self.dma("gp", wdst, wsrc, [texp, self._wskew_war])
            self._fband_war = twr
            rtoks = []
            reg = [i for i, qt in enumerate(qts) if qt != 0]
            if reg:
                i0 = reg[0]
                rsrc = bass.AP(self.band_f, slot0 + i0 * 128 * BAND_L,
                               [[BAND_L - 1, 128], [128 * BAND_L, len(reg)],
                                [1, BAND_L]])
                rdst = bass.AP(self.wskew, i0 * BAND_L,
                               [[4 * BAND_L, 128], [BAND_L, len(reg)], [1, BAND_L]])
                rtoks.append(self.dma("gp", rdst, rsrc, [twr]))
            if 0 in qts:
                rsrc = bass.AP(self.band_f, slot0 + 128,
                               [[BAND_L - 1, 128], [1, 256]])
                rtoks.append(self.dma("gp", self.wskew[:, 0:256], rsrc, [twr]))
            smm, exps = [], []
            ab0 = hp * 4 * BAND_L
            for i, qt in enumerate(qts):
                qb = qt * 128
                wb = max(0, qb - 128)
                wd = min(S, qb + 256) - wb
                smm.append(self.mm(self.psA[:, i * 512: i * 512 + wd],
                                   self.q_pair[hs, qb:qb + 128],
                                   self.khi[hs, wb:wb + wd],
                                   dep + [self._psA_war], start=True, stop=True))
                exps.append(b.emit("act", [smm[i], self._auband_war],
                                   lambda e, i=i, wd=wd, ab0=ab0: e.activation(
                    self.au_band[:, ab0 + i * BAND_L: ab0 + i * BAND_L + wd],
                    self.psA[:, i * 512: i * 512 + wd], AF.Exp, scale=self.inv8)))
            self._psA_war = exps[-1]
            tmul = b.emit("dve", exps + rtoks,
                          lambda e, ab0=ab0, n=nqt * BAND_L: e.tensor_mul(
                              self.au_band[:, ab0:ab0 + n],
                              self.au_band[:, ab0:ab0 + n], self.wskew[:, 0:n]))
            self._wskew_war = tmul
            for i, qt in enumerate(qts):
                qb = qt * 128
                wb = max(0, qb - 128)
                wd = min(S, qb + 256) - wb
                nblk = wd // 128
                trs = [self.tr(self.psA[:, i * 512 + k * 64: i * 512 + k * 64 + 64
                                        ].bitcast(BF16),
                               self.au_band[:, ab0 + i * BAND_L + k * 128:
                                            ab0 + i * BAND_L + k * 128 + 128],
                               [tmul, self._abT_war, self._psA_war],
                               start=(k == 0), stop=(k == nblk - 1))
                       for k in range(nblk)]
                ev = b.emit("dve", trs, lambda e, i=i, wd=wd: e.tensor_copy(
                    self.abT[:, i * BAND_L: i * BAND_L + wd],
                    self.psA[:, i * 512: i * 512 + wd // 2].bitcast(BF16)))
                self._psA_war = ev
                for k in range(nblk):
                    vt = wb // 128 + k
                    toks_mm.append(self.mm(
                        self.psB[0:65, qb:qb + 128],
                        self.v_tm[:, vt * self.VW + hp * 134:
                                  vt * self.VW + hp * 134 + 65],
                        self.abT[:, i * BAND_L + k * 128: i * BAND_L + k * 128 + 128],
                        [ev, t_zero], start=False, stop=False))
                self._abT_war = toks_mm[-1]
            ring = hp + 2 * (qtg % 2)
            rbase = 128 + ring * 128 * 4 * BAND_L
            tabw = self.dma("gp",
                            bass.AP(self.auband_d, rbase,
                                    [[4 * BAND_L, 128], [1, nqt * BAND_L]]),
                            self.au_band[:, ab0:ab0 + nqt * BAND_L],
                            [tmul, self._bb_ring_war[ring]])
            self._auband_war = tabw
            bbts = []
            if reg:
                i0 = reg[0]
                src = bass.AP(self.auband_d, rbase + i0 * BAND_L + 97,
                              [[4 * BAND_L + 1, 128], [BAND_L, len(reg)], [1, 127]])
                dstb = bass.AP(self.bb, hp * 512 + i0 * 128,
                               [[2 * 4 * 128, 128], [128, len(reg)], [1, 127]])
                bbts.append(self.dma("gp", dstb, src, [tabw, self._bb_war]))
            if 0 in qts:
                src = bass.AP(self.auband_d, rbase - 31,
                              [[4 * BAND_L + 1, 128], [1, 127]])
                bbts.append(self.dma("gp", self.bb[:, hp * 512: hp * 512 + 127],
                                     src, [tabw, self._bb_war]))
            self._bb_ring_war[ring] = bbts[-1]
            tmk = list(bbts)
            if 0 in qts:
                tmk.append(b.emit("dve", bbts, lambda e: e.tensor_mul(
                    self.bb[:, hp * 512: hp * 512 + 127],
                    self.bb[:, hp * 512: hp * 512 + 127], self.mask_bb0[:, 0:127])))
            if (NT - 1) in qts:
                iL = qts.index(NT - 1)
                sl = slice(hp * 512 + iL * 128, hp * 512 + iL * 128 + 127)
                tmk.append(b.emit("dve", bbts, lambda e, sl=sl: e.tensor_mul(
                    self.bb[:, sl], self.bb[:, sl], self.mask_bbL[:, 0:127])))
            btr = [self.tr(self.psA[0:127, i * 512: i * 512 + 64].bitcast(BF16),
                           self.bb[:, hp * 512 + i * 128: hp * 512 + i * 128 + 127],
                           tmk + [self._psA_war], start=True, stop=True)
                   for i, qt in enumerate(qts)]
            psrc2 = self.psA[0:127, 0:nqt * 512].bitcast(BF16).rearrange(
                "x (q c) -> x q c", c=1024)[:, :, 0:128]
            bev = b.emit("dve", btr, lambda e, psrc2=psrc2, nqt=nqt: e.tensor_copy(
                self.battn[:, 0:nqt * 128], psrc2))
            self._psA_war = bev
            self._bb_war = bev
            toks_mm.append(self.mm(self.psB[0:64, qtg * 512: qtg * 512 + nqt * 128],
                                   self.rvb[:, :], self.battn[:, 0:nqt * 128],
                                   [bev, t_zero, self._battn_war],
                                   start=False, stop=False))
            self._battn_war = toks_mm[-1]
        return toks_mm

    def _main_head(self, p, hp, dep, t_zero):
        b, S, NT = self.b, self.S, self.NT
        hs = slice(hp * 64, hp * 64 + 64)
        toks_av = []
        exp_war = [None, None]
        for kt in range(NT):
            a = kt * 128
            mm_toks = []
            for nb in range(self.NB):
                c0, c1 = nb * 512, nb * 512 + 512
                hi_end = min(max(a + 32, c0), c1)
                pieces = []
                if hi_end > c0:
                    pieces.append((self.khi, c0, hi_end))
                if c1 > hi_end:
                    pieces.append((self.klo, hi_end, c1))
                for pi, (kbuf, s0, s1) in enumerate(pieces):
                    mm_toks.append(self.mm(self.psA[:, s0:s1],
                                           kbuf[hs, a:a + 128],
                                           self.q_pair[hs, s0:s1],
                                           dep + [self._psA_war],
                                           start=(pi == 0),
                                           stop=(pi == len(pieces) - 1)))
            pg = kt % 2
            au0 = pg * S
            texp = b.emit("act", mm_toks + [exp_war[pg]],
                          lambda e, au0=au0: e.activation(
                              self.au[:, au0:au0 + S], self.psA[:, 0:S],
                              AF.Exp, scale=self.inv8))
            self._psA_war = texp
            tmks = []
            hs0 = max(0, a - 96)
            if a + 32 > hs0:
                moff = hs0 - (a - 96)
                w = a + 32 - hs0
                tmks.append(b.emit("dve", [texp],
                                   lambda e, hs0=hs0, moff=moff, w=w, au0=au0:
                                   e.tensor_mul(
                    self.au[:, au0 + hs0: au0 + hs0 + w],
                    self.au[:, au0 + hs0: au0 + hs0 + w],
                    self.mask_hi[:, moff:moff + w])))
            ls1 = min(S, a + 160)
            if ls1 > a + 32:
                w = ls1 - (a + 32)
                tmks.append(b.emit("dve", [texp], lambda e, a=a, w=w, au0=au0:
                                   e.tensor_mul(
                    self.au[:, au0 + a + 32: au0 + a + 32 + w],
                    self.au[:, au0 + a + 32: au0 + a + 32 + w],
                    self.mask_lo[:, 0:w])))
            kt_av = []
            for nb in range(self.NB):
                c0, c1 = nb * 512, nb * 512 + 512
                hi_end = min(max(a + 32, c0), c1)
                segs = []
                if hi_end > c0:
                    segs.append((0, c0, hi_end))
                if c1 > hi_end:
                    segs.append((1, hi_end, c1))
                for var, s0, s1 in segs:
                    kt_av.append(self.mm(
                        self.psB[0:67, s0:s1],
                        self.v_tm[:, kt * self.VW + hp * 134 + var * 67:
                                  kt * self.VW + hp * 134 + var * 67 + 67],
                        self.au[:, au0 + s0: au0 + s1],
                        tmks + [texp, t_zero], start=False, stop=False))
            exp_war[pg] = kt_av[-1]
            toks_av += kt_av
        return toks_av

    def attention(self):
        b, S, NT, NP = self.b, self.S, self.NT, self.NP
        self._pair_war = None
        self._dbg_pair_wars = []
        self._wslice_war = [None, None]
        self._vwar = [None] * 4
        self._psA_war = None
        self._abT_war = None
        self._bb_war = None
        self._battn_war = None
        self._auband_war = None
        self._fband_war = None
        self._wskew_war = None
        self._bb_ring_war = [None] * 4
        self._ctmp_war = None
        self._ctx_war = None
        self._sums_war = None
        self.t_ctx_dma = []
        self.t_sums_dma = []
        for p in range(NP):
            tq, tk, tv = self._proj_pair(p)
            dep = tq + tk
            for hp in range(2):
                h = 2 * p + hp
                zts = []
                for nb in range(self.NB):
                    zts.append(self.mm(self.psB[0:67, nb * 512:(nb + 1) * 512],
                                       self.zcol67[:, :], self.zeros_b[:, :],
                                       tv + [self._ctx_war],
                                       start=True, stop=False))
                t_zero = zts[-1]
                band_toks = self._band_head(p, hp, dep, t_zero)
                main_toks = self._main_head(p, hp, dep, t_zero)
                cls = [self.mm(self.psB[0:67, nb * 512:(nb + 1) * 512],
                               self.zcol67[:, :], self.zeros_b[:, :],
                               band_toks[-1:] + main_toks[-1:] + band_toks[-3:],
                               start=False, stop=True)
                       for nb in range(self.NB)]
                tse = b.emit("dve", cls + main_toks[-2:] + band_toks[-2:] +
                             [self._sums_war],
                             lambda e: e.tensor_copy(self.sums3[64:67, 0:S],
                                                     self.psB[64:67, 0:S]))
                tsd = self.dma("gp", self.sums_d[3 * h:3 * h + 3, :],
                               self.sums3[64:67, 0:S], [tse])
                self._sums_war = tsd
                self.t_sums_dma.append(tsd)
                tce = b.emit("dve", [tse, self._ctmp_war], lambda e: e.tensor_copy(
                    self.ctmp[:, 0:S], self.psB[0:64, 0:S]))
                tcd = self.dma("gp",
                               self.ctx_d[p * 128 + hp * 64: p * 128 + hp * 64 + 64, :],
                               self.ctmp[:, 0:S], [tce])
                self._ctmp_war = tcd
                self.t_ctx_dma.append(tcd)
                self._ctx_war = tce
            self._pair_war = self._ctx_war
            if p == 0 and self.debug:
                tps = [self.tap("qp", self.q_pair[:, :], tq),
                       self.tap("khi", self.khi[:, :], tk),
                       self.tap("vtm", self.v_tm[:, :], tv),
                       self.tap("au", self.au[:, :], [self._ctx_war]),
                       self.tap("sums3", self.sums3[64:67, :], [self._ctx_war])]
                self._dbg_pair_wars = tps

    def post_attention(self):
        b, S, D, KC, NP = self.b, self.S, self.D, self.KC, self.NP
        _u = getattr(self, "p4_upto", 3)
        if _u == -1:
            dep0 = []
        elif _u == -2:
            dep0 = self.t_ctx_dma + self.t_sums_dma
        elif _u == -3:
            dep0 = self.t_w2t + self.t_w1t
        elif _u == -4:
            dep0 = [self._psA_war]
        elif _u == -5:
            dep0 = (self.t_ctx_dma + self.t_sums_dma +
                    self.t_w2t + self.t_w1t)
        elif _u == -6:
            dep0 = [self._psA_war] + self.t_ctx_dma + self.t_sums_dma
        elif _u == -7:
            dep0 = [self._psA_war] + self.t_w2t + self.t_w1t
        else:
            dep0 = ([self._psA_war] + self.t_ctx_dma +
                    self.t_sums_dma + self.t_w2t + self.t_w1t)
        self.load_x(to_xf=True, dep=dep0)
        self._psA_war = self.t_xf[-1]
        self._ctxc_war = [self.t_w1t[-1], self.t_w2t[-1]]
        if getattr(self, "p4_upto", 3) < 1:
            self.t_resid = list(self.t_xf)
            self.t_comb = []
            return
        tprev = None
        nstat_war = None
        rraw_war = None
        comb = []
        for p in range(NP):
            tg = [self.dma("sp", self.nstat[0:2, :],
                           bass.AP(self.sums_d, (6 * p + 1) * S, [[S, 2], [1, S]]),
                           self.t_sums_dma + [nstat_war]),
                  self.dma("sp", self.nstat[2:4, :],
                           bass.AP(self.sums_d, (6 * p + 4) * S, [[S, 2], [1, S]]),
                           self.t_sums_dma + [nstat_war]),
                  self.dma("sp", self.rstat_raw[0:2, :],
                           bass.AP(self.sums_d, 6 * p * S, [[3 * S, 2], [1, S]]),
                           self.t_sums_dma + [rraw_war])]
            trc = b.emit("dve", tg + [tprev], lambda e: e.reciprocal(
                self.rstat[0:2, :], self.rstat_raw[0:2, :]))
            rraw_war = trc
            tb_ = None
            for nb in range(self.NB):
                c0 = nb * 512
                f0 = self.mm(self.psA[0:128, c0:c0 + 512], self.sel2[:, :],
                             self.rstat[:, c0:c0 + 512],
                             [trc, tprev, self._psA_war], start=True, stop=True)
                rr = self.mm(self.psB[0:128, c0:c0 + 512], self.rv4[:, :],
                             self.nstat[0:4, c0:c0 + 512],
                             tg + [tprev, self._ctx_war, self._psA_war],
                             start=True, stop=True)
                nstat_war = rr
                pp = nb % 2
                tld = self.dma("sp", self.ctxc[:, pp * 512:(pp + 1) * 512],
                               self.ctx_d[p * 128:(p + 1) * 128, c0:c0 + 512],
                               self.t_ctx_dma + [self._ctxc_war[pp]])
                ta = b.emit("dve", [rr, tld, tb_], lambda e, pp=pp, c0=c0:
                            e.tensor_add(self.x_tm[:, 0:512],
                                         self.ctxc[:, pp * 512:(pp + 1) * 512],
                                         self.psB[0:128, c0:c0 + 512]))
                tb_ = b.emit("dve", [ta, f0], lambda e, pp=pp, c0=c0:
                             e.tensor_mul(self.ctxc[:, pp * 512:(pp + 1) * 512],
                                          self.x_tm[:, 0:512],
                                          self.psA[0:128, c0:c0 + 512]))
                tst = self.dma("sp", self.ctxn_d[p * 128:(p + 1) * 128, c0:c0 + 512],
                               self.ctxc[:, pp * 512:(pp + 1) * 512], [tb_])
                self._ctxc_war[pp] = tst
                comb.append(tst)
            tprev = comb[-1]
        self.t_comb = comb
        if getattr(self, "p4_upto", 3) < 2:
            self.t_resid = list(comb)
            return
        two = [self.dmaT(self.woT[:, kc * D:(kc + 1) * D],
                         self.stg_o[0:D, kc * 128:(kc + 1) * 128],
                         self.o_pend + self.t_w2t + self.t_w1t)
               for kc in range(KC)]
        self.t_resid = []
        for nb in range(self.NB):
            c0 = nb * 512
            zts = [self.mm(self._ps8acc(oc), self.zcol128[:, :], self.zeros_b[:, :],
                           comb + [self._resid_war(nb)], start=True, stop=False)
                   for oc in range(KC)]
            last_mm = {}
            for kc2 in range(KC):
                pp = kc2 % 2
                tld = self.dma("sp", self.ctxc[:, pp * 512:(pp + 1) * 512],
                               self.ctxn_d[kc2 * 128:(kc2 + 1) * 128, c0:c0 + 512],
                               comb + [self._ctxc_war[pp]])
                for oc in range(KC):
                    acc = self._ps8acc(oc)
                    last_mm[oc] = self.mm(
                        acc, self.woT[:, kc2 * D + oc * 128: kc2 * D + oc * 128 + 128],
                        self.ctxc[:, pp * 512:(pp + 1) * 512],
                        zts + two + [tld],
                        start=False, stop=(kc2 == KC - 1))
                self._ctxc_war[pp] = last_mm[KC - 1]
            for oc in range(KC):
                t = b.emit("dve", [last_mm[oc]] + self.t_xf,
                           lambda e, oc=oc, c0=c0, acc=self._ps8acc(oc):
                           e.scalar_tensor_tensor(
                               self.x_f[:, oc * S + c0: oc * S + c0 + 512],
                               acc, self.boc[:, oc:oc + 1],
                               self.x_f[:, oc * S + c0: oc * S + c0 + 512],
                               ALU.add, ALU.add))
                self.t_resid.append(t)

    def _ps8acc(self, oc):
        ps = self.psA if oc < 4 else self.psB
        k = oc % 4
        return ps[0:128, k * 512:(k + 1) * 512]

    def _resid_war(self, nb):
        if nb == 0:
            return None
        return self.t_resid[nb * self.KC - 1] if self.t_resid else None

    def layernorm(self, dep):
        b, S, D, KC = self.b, self.S, self.D, self.KC
        sum_toks = []
        half_war = [None, None]
        ci = 0
        for kc in range(KC):
            for nb in range(self.NB):
                c0 = nb * 512
                pp = ci % 2
                h0 = pp * 512
                ci += 1
                tsq = b.emit("act", dep + [half_war[pp]],
                             lambda e, kc=kc, c0=c0, h0=h0: e.activation(
                                 self.x_tm[:, h0:h0 + 512],
                                 self.x_f[:, kc * S + c0: kc * S + c0 + 512],
                                 AF.Square))
                t_s = self.mm(self.psA[0:1, c0:c0 + 512], self.ones_col_f[:, :],
                              self.x_f[:, kc * S + c0: kc * S + c0 + 512],
                              dep, start=(kc == 0), stop=(kc == KC - 1))
                t_q = self.mm(self.psB[0:1, c0:c0 + 512], self.ones_col_f[:, :],
                              self.x_tm[:, h0:h0 + 512], [tsq],
                              start=(kc == 0), stop=(kc == KC - 1))
                half_war[pp] = t_q
                if kc == KC - 1:
                    sum_toks += [t_s, t_q]

        t_r0 = b.emit("dve", sum_toks, lambda e: (
            e.tensor_scalar(self.srow[:, 0:S], self.psA[0:1, 0:S], 1.0 / D, None,
                            ALU.mult),
            e.tensor_scalar(self.rowB[:, 0:S], self.psB[0:1, 0:S], 1.0 / D, None,
                            ALU.mult))[-1])
        t_r1 = b.emit("dve", [t_r0], lambda e: e.tensor_mul(
            self.arow[:, 0:S], self.srow[:, 0:S], self.srow[:, 0:S]))
        t_r = b.emit("dve", [t_r1], lambda e: e.tensor_sub(
            self.rowB[:, 0:S], self.rowB[:, 0:S], self.arow[:, 0:S]))
        t_ln = b.emit("act", [t_r], lambda e: e.activation(
            self.arow[:, 0:S], self.rowB[:, 0:S], AF.Ln, bias=self.eps_col[0:1, :]))
        t_ex = b.emit("act", [t_ln], lambda e: e.activation(
            self.rowB[:, 0:S], self.arow[:, 0:S], AF.Exp, scale=-0.5))
        t_nm = b.emit("dve", [t_ex], lambda e: e.tensor_mul(
            self.arow[:, 0:S], self.srow[:, 0:S], self.rowB[:, 0:S]))
        t_ng = b.emit("dve", [t_nm], lambda e: e.tensor_scalar(
            self.arow[:, 0:S], self.arow[:, 0:S], -1.0, None, ALU.mult))
        bc = []
        for nb in range(self.NB):
            c0 = nb * 512
            bc.append(self.mm(self.psA[0:128, c0:c0 + 512], self.ones_r1f[:, :],
                              self.rowB[:, c0:c0 + 512], [t_ex, sum_toks[-1], t_ng],
                              start=True, stop=True))
            bc.append(self.mm(self.psB[0:128, c0:c0 + 512], self.ones_r1f[:, :],
                              self.arow[:, c0:c0 + 512], [t_ng],
                              start=True, stop=True))
        out_toks = []
        prev = None
        ci = 0
        for kc in range(KC):
            for nb in range(self.NB):
                c0 = nb * 512
                h0 = (ci % 2) * 512
                ci += 1
                sl = slice(kc * S + c0, kc * S + c0 + 512)
                ta = b.emit("dve", bc + [t_r, prev], lambda e, sl=sl, h0=h0, c0=c0:
                            e.tensor_mul(self.x_tm[:, h0:h0 + 512], self.x_f[:, sl],
                                         self.psA[0:128, c0:c0 + 512]))
                prev = b.emit("dve", [ta], lambda e, sl=sl, h0=h0, c0=c0:
                              e.tensor_add(self.x_f[:, sl],
                                           self.x_tm[:, h0:h0 + 512],
                                           self.psB[0:128, c0:c0 + 512]))
                out_toks.append(prev)
        return out_toks

    def quant1(self, dep):
        b, S, KC, NT = self.b, self.S, self.KC, self.NT
        reds = []
        red_t = []
        half_war = [None, None]
        for nb in range(self.NB):
            c0 = nb * 512
            pp = nb % 2
            h0 = pp * 512
            t = b.emit("dve", dep + [half_war[pp]], lambda e, c0=c0, h0=h0:
                       e.scalar_tensor_tensor(
                           self.x_tm[:, h0:h0 + 512], self.x_f[:, c0:c0 + 512],
                           -1.0, self.x_f[:, c0:c0 + 512], ALU.mult, ALU.max))
            for kc in range(1, KC):
                sl = slice(kc * S + c0, kc * S + c0 + 512)
                t = b.emit("dve", [t], lambda e, sl=sl, h0=h0: e.tensor_tensor(
                    self.x_tm[:, h0:h0 + 512], self.x_tm[:, h0:h0 + 512],
                    self.x_f[:, sl], ALU.max))
                t = b.emit("dve", [t], lambda e, sl=sl, h0=h0:
                           e.scalar_tensor_tensor(
                               self.x_tm[:, h0:h0 + 512], self.x_f[:, sl], -1.0,
                               self.x_tm[:, h0:h0 + 512], ALU.mult, ALU.max))
            for j in range(4):
                tt = nb * 4 + j
                bank = self.psA[:, (tt % 4) * 512:(tt % 4) * 512 + 128]
                war = reds[tt - 4] if tt >= 4 else None
                ttr = self.tr(bank, self.x_tm[:, h0 + j * 128: h0 + j * 128 + 128],
                              [t, war], f32=True)
                red_t.append(ttr)
                reds.append(b.emit("dve", [ttr], lambda e, tt=tt, bank=bank:
                                   e.reduce_max(self.stat[:, tt:tt + 1], bank,
                                                axis=mybir.AxisListType.X)))
            half_war[pp] = red_t[-1]

        tsa = b.emit("dve", reds + self.t_wq_done, lambda e: e.tensor_scalar(
            self.stat[:, 16:16 + NT], self.stat[:, 0:NT], 1e-5, None, ALU.max))
        tdv = b.emit("dve", [tsa], lambda e: e.reciprocal(
            self.stat[:, 32:32 + NT], self.stat[:, 16:16 + NT]))
        tdv = b.emit("dve", [tdv], lambda e: e.tensor_scalar(
            self.stat[:, 32:32 + NT], self.stat[:, 32:32 + NT],
            127.0, None, ALU.mult))
        tsb = b.emit("dve", [tdv], lambda e: e.tensor_scalar(
            self.stat[:, 32 + NT:32 + 2 * NT], self.stat[:, 16:16 + NT],
            self.swc[:, 1:2], None, ALU.mult))
        ttr = self.tr(self.psB[0:2 * NT, 0:128], self.stat[:, 32:32 + 2 * NT],
                      [tsb], f32=True)
        tev = b.emit("dve", [ttr], lambda e: e.tensor_copy(
            self.shift32[0:2 * NT, 0:128], self.psB[0:2 * NT, 0:128]))
        td1 = self.dma("gp", self.srow[:, :].rearrange("p (k c) -> p k c", c=128),
                       self.shift32[0:NT, 0:128], [tev], s2s=True)
        td2 = self.dma("gp", self.arow[:, :].rearrange("p (k c) -> p k c", c=128),
                       self.shift32[NT:2 * NT, 0:128],
                       [tev], s2s=True)
        out = []
        ci = 0
        for nb in range(self.NB):
            c0 = nb * 512
            t_sb = self.mm(self.psA[0:128, c0:c0 + 512], self.ones_r1f[:, :],
                           self.srow[:, c0:c0 + 512], [td1, tev, reds[-1]],
                           start=True, stop=True)
            t_ab = self.mm(self.psB[0:128, c0:c0 + 512], self.ones_r1f[:, :],
                           self.arow[:, c0:c0 + 512], [td2, ttr],
                           start=True, stop=True)
            prevq = None
            for kc in range(KC):
                sl = slice(kc * S + c0, kc * S + c0 + 512)
                h0 = (ci % 2) * 512
                ci += 1
                ta = b.emit("dve", [t_sb, prevq], lambda e, sl=sl, c0=c0, h0=h0:
                            e.tensor_mul(self.x_tm[:, h0:h0 + 512], self.x_f[:, sl],
                                         self.psA[0:128, c0:c0 + 512]))
                tb_ = b.emit("dve", [ta], lambda e, h0=h0: e.tensor_scalar(
                    self.x_tm[:, h0:h0 + 512], self.x_tm[:, h0:h0 + 512],
                    RND_C, RND_C, ALU.add, ALU.subtract))
                prevq = b.emit("dve", [tb_, t_ab], lambda e, sl=sl, c0=c0, h0=h0:
                               e.tensor_mul(self.xbf[:, sl],
                                            self.x_tm[:, h0:h0 + 512],
                                            self.psB[0:128, c0:c0 + 512]))
                out.append(prevq)
        return out

    def ffn(self, dep, t_park):
        b, S, D, DH, KC, HC = self.b, self.S, self.D, self.DH, self.KC, self.HC
        RW, NREG = self.RW, self.NREG
        CB_H = 4 * 130
        CB_Q = 5 * 130
        CB_D = 6 * 130
        tw1 = [self.dma("sp", self.w1T[:, kc * DH:(kc + 1) * DH],
                        self.stg_w1t[kc * 128:(kc + 1) * 128, :],
                        self.t_w1t + [t_park] + dep) for kc in range(KC)]
        resid_out = []
        w2last = None
        habs_war = None
        xq2war = [None, None]
        w2cwar = [None, None]
        xq2dve = [None, None]
        for r in range(NREG):
            t0 = r * RW
            e0 = max(0, t0 - 1)
            e1 = min(S, t0 + RW + 1)
            EW = e1 - e0
            o0 = t0 - e0
            txl = self.dma("sp",
                           self.x_tm[:, 0:D].rearrange("p (k c) -> p k c", c=RW),
                           bass.AP(self.x_d, t0, [[KC * S, 128], [S, KC], [1, RW]]),
                           [t_park, w2last] + dep)
            conv_done = {}
            habs = None
            for hc in range(HC):
                slot = self.psA[:, (hc % 4) * 512:(hc % 4) * 512 + EW]
                last = None
                for kc in range(KC):
                    last = self.mm(slot,
                                   self.w1T[:, kc * DH + hc * 128:
                                            kc * DH + hc * 128 + 128],
                                   self.xbf[:, kc * S + e0: kc * S + e1],
                                   tw1 + dep + [conv_done.get(hc - 4), w2last],
                                   start=(kc == 0), stop=(kc == KC - 1))
                c1s = (hc % 2) * 2 * 130
                c2s = c1s + 130
                if o0 == 0:
                    ta = b.emit("dve", [last, conv_done.get(hc - 2)],
                                lambda e, slot=slot, hc=hc, c1s=c1s: (
                        e.tensor_scalar(self.cbuf[:, c1s:c1s + 1],
                                        self.cbc[:, hc:hc + 1], 1.0, None, ALU.mult),
                        e.tensor_scalar(self.cbuf[:, c1s + 1:c1s + RW],
                                        slot[:, 0:RW - 1],
                                        self.cwc[:, hc * 3:hc * 3 + 1],
                                        self.cbc[:, hc:hc + 1],
                                        ALU.mult, ALU.add))[-1])
                else:
                    ta = b.emit("dve", [last, conv_done.get(hc - 2)],
                                lambda e, slot=slot, hc=hc, c1s=c1s, o0=o0:
                                e.tensor_scalar(self.cbuf[:, c1s:c1s + RW],
                                                slot[:, o0 - 1:o0 - 1 + RW],
                                                self.cwc[:, hc * 3:hc * 3 + 1],
                                                self.cbc[:, hc:hc + 1],
                                                ALU.mult, ALU.add))
                tb_ = b.emit("dve", [ta], lambda e, slot=slot, hc=hc, c1s=c1s,
                             c2s=c2s, o0=o0: e.scalar_tensor_tensor(
                    self.cbuf[:, c2s:c2s + RW], slot[:, o0:o0 + RW],
                    self.cwc[:, hc * 3 + 1:hc * 3 + 2], self.cbuf[:, c1s:c1s + RW],
                    ALU.mult, ALU.add))
                if e1 == S:
                    tc_ = b.emit("dve", [tb_], lambda e, slot=slot, hc=hc, c1s=c1s,
                                 c2s=c2s, o0=o0: (
                        e.scalar_tensor_tensor(
                            self.cbuf[:, c1s:c1s + RW - 1],
                            slot[:, o0 + 1:o0 + RW],
                            self.cwc[:, hc * 3 + 2:hc * 3 + 3],
                            self.cbuf[:, c2s:c2s + RW - 1], ALU.mult, ALU.add),
                        e.tensor_copy(self.cbuf[:, c1s + RW - 1:c1s + RW],
                                      self.cbuf[:, c2s + RW - 1:c2s + RW]))[-1])
                else:
                    tc_ = b.emit("dve", [tb_], lambda e, slot=slot, hc=hc, c1s=c1s,
                                 c2s=c2s, o0=o0: e.scalar_tensor_tensor(
                        self.cbuf[:, c1s:c1s + RW], slot[:, o0 + 1:o0 + 1 + RW],
                        self.cwc[:, hc * 3 + 2:hc * 3 + 3],
                        self.cbuf[:, c2s:c2s + RW], ALU.mult, ALU.add))
                conv_done[hc] = tc_
                tsil = b.emit("act", [tc_], lambda e, hc=hc, c1s=c1s: e.activation(
                    self.h2reg[:, hc * RW:(hc + 1) * RW], self.cbuf[:, c1s:c1s + RW],
                    AF.Silu))
                if hc == 0:
                    habs = b.emit("dve", [tsil, habs_war], lambda e, hc=hc:
                                  e.scalar_tensor_tensor(
                        self.cbuf[:, CB_H:CB_H + RW],
                        self.h2reg[:, hc * RW:(hc + 1) * RW], -1.0,
                        self.h2reg[:, hc * RW:(hc + 1) * RW], ALU.mult, ALU.max))
                else:
                    habs = b.emit("dve", [tsil, habs], lambda e, hc=hc:
                                  e.tensor_tensor(
                        self.cbuf[:, CB_H:CB_H + RW], self.cbuf[:, CB_H:CB_H + RW],
                        self.h2reg[:, hc * RW:(hc + 1) * RW], ALU.max))
                    habs = b.emit("dve", [habs], lambda e, hc=hc:
                                  e.scalar_tensor_tensor(
                        self.cbuf[:, CB_H:CB_H + RW],
                        self.h2reg[:, hc * RW:(hc + 1) * RW],
                        -1.0, self.cbuf[:, CB_H:CB_H + RW], ALU.mult, ALU.max))
            bank = self.psA[0:128, 0:128]
            ttr = self.tr(bank, self.cbuf[:, CB_H:CB_H + RW],
                          [habs, conv_done[HC - 1]], f32=True)
            habs_war = ttr
            red = b.emit("dve", [ttr], lambda e, bank=bank:
                         e.reduce_max(self.stat[:, 0:1], bank,
                                      axis=mybir.AxisListType.X))
            tsa = b.emit("dve", [red], lambda e: e.tensor_scalar(
                self.stat[:, 4:5], self.stat[:, 0:1], 1e-5, None, ALU.max))
            tdv = b.emit("dve", [tsa], lambda e: e.reciprocal(
                self.stat[:, 8:9], self.stat[:, 4:5]))
            tdv = b.emit("dve", [tdv], lambda e: e.tensor_scalar(
                self.stat[:, 8:9], self.stat[:, 8:9], 127.0, None, ALU.mult))
            tsb = b.emit("dve", [tdv], lambda e: e.tensor_scalar(
                self.stat[:, 9:10], self.stat[:, 4:5],
                self.swc[:, 3:4], None, ALU.mult))
            ttr2 = self.tr(self.psB[0:2, 0:128], self.stat[:, 8:10],
                           [tsb, self._ctx_free], f32=True)
            tev = b.emit("dve", [ttr2], lambda e: e.tensor_copy(
                self.shift32[0:2, 0:128], self.psB[0:2, 0:128]))
            td1 = self.dma("gp", self.srow[:, 0:128], self.shift32[0:1, 0:128],
                           [tev], s2s=True)
            td2 = self.dma("gp", self.arow[:, 0:128], self.shift32[1:2, 0:128],
                           [tev], s2s=True)
            t_sb = self.mm(self.psA[0:128, 0:RW], self.ones_r1f[:, :],
                           self.srow[:, 0:RW], [td1, red], start=True, stop=True)
            t_ab = self.mm(self.psA[0:128, 512:512 + RW], self.ones_r1f[:, :],
                           self.arow[:, 0:RW], [td2, red], start=True, stop=True)
            ta2 = b.emit("dve", [t_ab], lambda e: e.tensor_copy(
                self.a2sb[:, 0:RW], self.psA[0:128, 512:512 + RW]))
            zts = [self.mm(self.psB[0:128, bank * 512:(bank + 1) * 512],
                           self.zcol128[:, :], self.zeros_b[:, :],
                           [tev, w2cwar[0], w2cwar[1]], start=True, stop=False)
                   for bank in range((KC + 1) // 2)]
            for hc in range(HC):
                pp = hc % 2
                xs = pp * 128
                tq1 = b.emit("dve", [t_sb, xq2war[pp], ta2,
                                     xq2dve[0], xq2dve[1]],
                             lambda e, hc=hc: e.tensor_mul(
                                 self.cbuf[:, CB_Q:CB_Q + RW],
                                 self.h2reg[:, hc * RW:(hc + 1) * RW],
                                 self.psA[0:128, 0:RW]))
                tq2 = b.emit("dve", [tq1, xq2war[pp]],
                             lambda e, xs=xs: e.tensor_scalar(
                                 self.xq2c[:, xs:xs + RW],
                                 self.cbuf[:, CB_Q:CB_Q + RW], RND_C, RND_C,
                                 ALU.add, ALU.subtract))
                tl = self.dma("sp", self.w2Tc[:, pp * D:(pp + 1) * D],
                              self.stg_w2t[hc * 128:(hc + 1) * 128, :],
                              self.t_w2t + [w2cwar[pp]])
                for oc in range(KC):
                    slot = self.psB[0:128, (oc // 2) * 512 + (oc % 2) * 256:
                                    (oc // 2) * 512 + (oc % 2) * 256 + RW]
                    tmm = self.mm(slot,
                                  self.w2Tc[:, pp * D + oc * 128:
                                            pp * D + oc * 128 + 128],
                                  self.xq2c[:, xs:xs + RW], zts + [tq2, tl],
                                  start=False,
                                  stop=(hc == HC - 1 and
                                        (oc % 2 == 1 or oc == KC - 1)))
                w2cwar[pp] = tmm
                xq2war[pp] = tmm
                xq2dve[pp] = tq2
            tprev = txl
            for oc in range(KC):
                slot = self.psB[0:128, (oc // 2) * 512 + (oc % 2) * 256:
                                (oc // 2) * 512 + (oc % 2) * 256 + RW]
                ta = b.emit("dve", [w2cwar[0], w2cwar[1], ta2,
                                    xq2dve[0], xq2dve[1], tprev],
                            lambda e, slot=slot: e.tensor_mul(
                                self.cbuf[:, CB_D:CB_D + RW], slot,
                                self.a2sb[:, 0:RW]))
                tprev = b.emit("dve", [ta, txl], lambda e, oc=oc: e.tensor_add(
                    self.x_tm[:, oc * RW:(oc + 1) * RW],
                    self.cbuf[:, CB_D:CB_D + RW],
                    self.x_tm[:, oc * RW:(oc + 1) * RW]))
            tst = self.dma("sp",
                           bass.AP(self.x_d, t0, [[KC * S, 128], [S, KC], [1, RW]]),
                           self.x_tm[:, 0:D].rearrange("p (k c) -> p k c", c=RW),
                           [tprev])
            resid_out.append(tst)
            w2last = tst
        trl = self.dma("sp", self.x_f[:, :], self.x_d[:, :], resid_out)
        return [trl]

    def output(self, dep):
        b, S, KC, NT = self.b, self.S, self.KC, self.NT
        prev_store = None
        for tt in range(NT):
            evs = []
            for kc in range(KC):
                ps = self.psA if kc < 4 else self.psB
                bank = ps[:, (kc % 4) * 512:(kc % 4) * 512 + 128]
                ttr = self.tr(bank,
                              self.x_f[:, kc * S + tt * 128: kc * S + tt * 128 + 128],
                              dep + [prev_store], start=True,
                              stop=True, f32=True)
                evs.append(b.emit("dve", [ttr], lambda e, kc=kc, bank=bank:
                                  e.tensor_copy(self.x_tm[:, kc * 128:(kc + 1) * 128],
                                                bank)))
            prev_store = self.dma("sp", self.out_d[tt * 128:(tt + 1) * 128, :],
                                  self.x_tm[:, 0:self.D], evs)
        self.t_out = prev_store

    def build_all(self, stage=9):
        self.consts()
        self.small_inputs()
        self.load_x()
        dep = self.t_x + self.t_small
        if stage >= 2:
            self.weights_qkvo()
            self.weights_w12()
            dep = (dep + self.t_wq_done + self.t_w2t + self.t_w1t +
                   [t for v in self.t_wT.values() for t in v])
        if stage >= 3:
            self.attention()
            dep = self.t_ctx_dma + self.t_sums_dma
        if stage >= 4:
            self.post_attention()
            self._ctx_free = self.t_resid[-1]
            tp1 = self.tap("xattn", self.x_f[:, :], self.t_resid)
            tp2 = self.tap("ctxn",
                           bass.AP(self.ctxn_d, 0,
                                   [[self.S, 128], [128 * self.S, self.NP],
                                    [1, self.S]]),
                           self.t_comb)
            dep = self.t_resid + [tp1, tp2]
        if stage >= 5:
            t_ln1 = self.layernorm(dep)
            tp3 = self.tap("x1", self.x_f[:, :], t_ln1)
            dep = t_ln1 + [tp3]
        if stage >= 6:
            tq1 = self.quant1(dep)
            tp4 = self.tap("xq1", self.xbf[:, :], tq1)
            dep = tq1 + [tp4]
        if stage >= 7:
            t_park = self.dma("sp", self.x_d[:, :], self.x_f[:, :], dep)
            dep = self.ffn(dep, t_park)
        if stage >= 8:
            dep = self.layernorm(dep)
        self.output(dep)
        fin = [self.t_out] + self.b.dma_totals("sp") + self.b.dma_totals("gp")
        self.b.emit("dve", fin, lambda e: None, inc=0)
        nc = self.nc
        from contextlib import ExitStack
        names = ["pe", "act", "dve", "gpc"] + \
            [f"sp{r}" for r in range(Builder.NRING)] + \
            [f"gp{r}" for r in range(Builder.NRING)]
        with ExitStack() as ctx:
            sems = {n: ctx.enter_context(nc.semaphore(f"s_{n}")) for n in names}
            block = ctx.enter_context(nc.Block())
            self.b.build(block, sems)
        return nc


_PROG_CACHE = {}


def get_nc(S, D, H, DH):
    key = (S, D, H, DH)
    if key not in _PROG_CACHE:
        _PROG_CACHE[key] = Prog(S, D, H, DH).build_all()
    return _PROG_CACHE[key]


def kernel(L0_attn_x, L0_attn_Wq, L0_attn_Wk, L0_attn_Wv, L0_attn_Wo, L0_attn_b_o,
           L0_attn_rel_k_emb, L0_attn_rel_v_emb, L0_ffn_w1, L0_ffn_conv_w,
           L0_ffn_conv_b, L0_ffn_w2):
    x = np.asarray(L0_attn_x)
    B, S, D = x.shape
    DH = np.asarray(L0_ffn_w1).shape[0]
    nc = get_nc(S, D, 16, DH)
    f = np.float32
    shared = {
        "wq_in": np.ascontiguousarray(L0_attn_Wq, f),
        "wk_in": np.ascontiguousarray(L0_attn_Wk, f),
        "wv_in": np.ascontiguousarray(L0_attn_Wv, f),
        "wo_in": np.ascontiguousarray(L0_attn_Wo, f),
        "bo_in": np.ascontiguousarray(L0_attn_b_o, f).reshape(1, D),
        "rk_in": np.ascontiguousarray(L0_attn_rel_k_emb, f),
        "rv_in": np.ascontiguousarray(L0_attn_rel_v_emb, f),
        "w1_in": np.ascontiguousarray(L0_ffn_w1, f),
        "cw_in": np.ascontiguousarray(L0_ffn_conv_w, f).reshape(DH, 3),
        "cb_in": np.ascontiguousarray(L0_ffn_conv_b, f).reshape(1, DH),
        "w2_in": np.ascontiguousarray(L0_ffn_w2, f),
    }
    in_maps = []
    for c in range(B):
        m = dict(shared)
        m["x_in"] = np.ascontiguousarray(x[c], f)
        in_maps.append(m)
    res = run_bass_kernel_spmd(nc, in_maps, core_ids=list(range(B)))
    return np.stack([np.asarray(r["out"], dtype=np.float32) for r in res.results])


# version 11
# speedup vs baseline: 1.1160x; 1.1160x over previous
import math
import numpy as np
import ml_dtypes
import concourse.bass as bass
import concourse.mybir as mybir
from concourse.bass_utils import run_bass_kernel_spmd

F32 = mybir.dt.float32
BF16 = mybir.dt.bfloat16
AF = mybir.ActivationFunctionType
ALU = mybir.AluOpType
RND_C = 1.5 * 2 ** 23
ENGINES = ("pe", "act", "dve", "sp", "gp")
BAND_L = 384


class Builder:
    NRING = 4
    FOLD_LIM = 12

    def __init__(self):
        self.q = {e: [] for e in ENGINES}
        self.val = {}
        self.ring_idx = {"sp": 0, "gp": 0}
        self.relay_fn = None
        self.relay_fns = None
        self._relay_ring = [None] * 8
        self._relay_i = 0
        self._in_relay = False

    def _relay(self, ws):
        k = self._relay_i % 8
        self._relay_i += 1
        fn = self.relay_fns[k] if self.relay_fns else self.relay_fn
        self._in_relay = True
        rtok = self.emit("dve", list(ws) + [self._relay_ring[k]], fn)
        self._in_relay = False
        self._relay_ring[k] = rtok
        return rtok

    def _bump(self, sem, inc):
        self.val[sem] = self.val.get(sem, 0) + inc
        return (sem, self.val[sem])

    def emit(self, eng, waits, fn, inc=1):
        coll = {}
        for w in waits:
            if w is not None and coll.get(w[0], -1) < w[1]:
                coll[w[0]] = w[1]
        waits = [(s, v) for s, v in coll.items()]
        if eng == "act" and self.relay_fn is not None:
            pe_waits = [w for w in waits if w[0] == "pe"]
            if pe_waits:
                waits = [w for w in waits if w[0] != "pe"]
                waits.append(self._relay(pe_waits))
        if self.relay_fn is not None and not self._in_relay:
            while len(waits) > self.FOLD_LIM:
                head, waits = waits[:self.FOLD_LIM], waits[self.FOLD_LIM:]
                waits = waits + [self._relay(head)]
        if eng in ("sp", "gp"):
            r = self.ring_idx[eng]
            self.ring_idx[eng] = (r + 1) % self.NRING
            sem = f"{eng}{r}"
            prev = self.val.get(sem, 0)
            if prev:
                waits.append((sem, prev))
            tok = self._bump(sem, 16)
            self.q[eng].append((waits, fn, 16, sem))
            return tok
        sem = eng
        tok = self._bump(sem, inc)
        q = "gp" if eng == "gpc" else eng
        self.q[q].append((waits, fn, inc, sem))
        return tok

    def dma_totals(self, eng):
        return [(f"{eng}{r}", self.val.get(f"{eng}{r}", 0))
                for r in range(self.NRING) if self.val.get(f"{eng}{r}", 0)]

    def build(self, block, sems):
        def mk(eng_name):
            def body(engine):
                waited = {}
                for waits, fn, inc, sem_name in self.q[eng_name]:
                    for (sname, sval) in waits:
                        if waited.get(sname, -1) < sval:
                            engine.wait_ge(sems[sname], sval)
                            waited[sname] = sval
                    inst = fn(engine)
                    if inc > 0:
                        assert inst is not None, f"{eng_name} step returned None"
                        inst.then_inc(sems[sem_name], inc)
            return body
        block.tensor(mk("pe"))
        block.scalar(mk("act"))
        block.vector(mk("dve"))
        block.sync(mk("sp"))
        block.gpsimd(mk("gp"))


class Prog:
    def __init__(self, S=2048, D=1024, H=16, DH=4096, debug=False):
        self.debug = debug
        self.S, self.D, self.H, self.DH, self.DK = S, D, H, DH, 64
        self.NT, self.KC, self.HC, self.NP = S // 128, D // 128, DH // 128, H // 2
        self.NB = S // 512
        self.RW = 128
        self.NREG = S // self.RW
        assert self.NP == self.KC
        self.inv8 = 0.125
        self._tr_tok = None
        self._s2s_tok = None
        self.nc = bass.Bass("TRN2", target_bir_lowering=False)
        self.b = Builder()
        self._declare_dram()
        self._alloc_sbuf()

    def _declare_dram(self):
        nc, S, D, DH = self.nc, self.S, self.D, self.DH
        dt = nc.dram_tensor
        self.x_in = dt("x_in", [S, D], F32, kind="ExternalInput")
        self.wq_in = dt("wq_in", [D, D], F32, kind="ExternalInput")
        self.wk_in = dt("wk_in", [D, D], F32, kind="ExternalInput")
        self.wv_in = dt("wv_in", [D, D], F32, kind="ExternalInput")
        self.wo_in = dt("wo_in", [D, D], F32, kind="ExternalInput")
        self.bo_in = dt("bo_in", [1, D], F32, kind="ExternalInput")
        self.rk_in = dt("rk_in", [65, 64], F32, kind="ExternalInput")
        self.rv_in = dt("rv_in", [65, 64], F32, kind="ExternalInput")
        self.w1_in = dt("w1_in", [DH, D], F32, kind="ExternalInput")
        self.cw_in = dt("cw_in", [DH, 3], F32, kind="ExternalInput")
        self.cb_in = dt("cb_in", [1, DH], F32, kind="ExternalInput")
        self.w2_in = dt("w2_in", [D, DH], F32, kind="ExternalInput")
        self.out_d = dt("out", [S, D], F32, kind="ExternalOutput")
        self.stg_q = dt("stg_q", [D, D], BF16)
        self.stg_k = dt("stg_k", [D, D], BF16)
        self.stg_v = dt("stg_v", [D, D], BF16)
        self.stg_o = dt("stg_o", [D, D], BF16)
        self.stg_w1 = dt("stg_w1", [DH, D], BF16)
        self.stg_w2 = dt("stg_w2", [D, DH], BF16)
        self.stg_w2t = dt("stg_w2t", [DH, D], BF16)
        self.stg_w1t = dt("stg_w1t", [D, DH], BF16)
        self.stg_qt = dt("stg_qt", [D, D], BF16)
        self.stg_kt = dt("stg_kt", [D, D], BF16)
        self.stg_vt = dt("stg_vt", [D, D], BF16)
        self.ctx_d = dt("ctx_d", [D, S], BF16)
        self.ctxn_d = dt("ctxn_d", [D, S], BF16)
        self.x_d = dt("x_d", [128, (D // 128) * S], F32)
        self.sums_d = dt("sums_d", [3 * self.H, S], F32)
        self.band_f = dt("band_f", [1, 8 * 128 * BAND_L], BF16)
        self.auband_d = dt("auband_d", [1, 128 + 4 * 128 * 4 * BAND_L], BF16)
        self.dbg_d = {}

    def tap(self, name, ap, waits):
        if not self.debug:
            return None
        shape = [ap.shape[0], int(np.prod(ap.shape[1:]))]
        t = self.nc.dram_tensor("dbg_" + name, shape, ap.dtype,
                                kind="ExternalOutput")
        self.dbg_d[name] = shape
        return self.dma("sp", t[:, :], ap, waits)

    def _alloc_sbuf(self):
        nc = self.nc
        S, D, DH, KC, NT, HC, NP, H = (self.S, self.D, self.DH, self.KC, self.NT,
                                       self.HC, self.NP, self.H)
        ARENA = 200 * 1024
        arena = nc.alloc_sbuf_tensor("arena", [128, ARENA], mybir.dt.uint8)
        base = nc.lookup_mloc(arena).addr
        self._off = base
        self._base = base
        self._offs = {}

        def at(name, shape, dtype, alias=None):
            nbytes = int(np.prod(shape[1:])) * mybir.dt.size(dtype)
            nbytes = (nbytes + 31) & ~31
            off = self._off if alias is None else alias
            if alias is None:
                self._off += nbytes
            self._offs[name] = (off, off + nbytes)
            t = nc.alloc_sbuf_tensor_at(name, shape, dtype, offset=off)
            return t

        self.xbf = at("xbf", [128, KC * S], BF16)
        self.q_pair = at("q_pair", [128, S], BF16)
        self.khi = at("khi", [128, S], BF16)
        self.klo = at("klo", [128, S], BF16)
        self.VW = 2 * 2 * 67
        self.v_tm = at("v_tm", [128, NT * self.VW], BF16)
        self.au = at("au", [128, 2 * S], BF16)
        self.wqp = at("wqp", [128, 2 * KC * 128], BF16)
        self.wkp = at("wkp", [128, 2 * KC * 128], BF16)
        self.wvp = at("wvp", [128, 2 * KC * 128], BF16)
        self.fband = at("fband", [128, 4 * BAND_L], BF16)
        self.wskew = at("wskew", [128, 4 * BAND_L], BF16)
        self.au_band = at("au_band", [128, 2 * 4 * BAND_L], BF16)
        self.abT = at("abT", [128, 4 * BAND_L], BF16)
        self.bb = at("bb", [128, 2 * 4 * 128], BF16)
        self.battn = at("battn", [127, 4 * 128], BF16)
        attn_pool_start = self._offs["q_pair"][0]
        attn_pool_end = self._off
        self.wstg = at("wstg", [128, 2 * 2048], F32)
        self.wbf16 = at("wbf16", [128, 2 * 2048], BF16)
        _need = max(KC * S * 4 + KC * D * 2 + 2048,
                    KC * DH * 2 + HC * 128 * 4 + 8 * 130 * 4)
        _nat = self._off - attn_pool_start
        self.poolpad = at("poolpad", [128, max(32, _need - _nat)],
                          mybir.dt.uint8)
        wstg_start = self._offs["wstg"][0]
        self.rv4 = at("rv4", [4, 128], F32)
        self.sel2 = at("sel2", [2, 128], F32)
        self.ident = at("ident", [128, 128], BF16)
        self.identf = at("identf", [128, 128], F32)
        self.ones_col_f = at("ones_col_f", [128, 1], F32)
        self.ones_r1f = at("ones_r1f", [1, 128], F32)
        self.onespadB = at("onespadB", [1, 128], F32)
        self.onespadA = at("onespadA", [1, 128], F32)
        self.zeros_b = at("zeros_b", [1, 512], BF16)
        self.zcol67 = at("zcol67", [1, 67], BF16)
        self.zcol128 = at("zcol128", [1, 128], BF16)
        self.mask_hi = at("mask_hi", [128, 128], BF16)
        self.mask_lo = at("mask_lo", [128, 128], BF16)
        self.mask_bb0 = at("mask_bb0", [128, 127], BF16)
        self.mask_bbL = at("mask_bbL", [128, 127], BF16)
        self.rkq = at("rkq", [128, 65], BF16)
        self.rvb = at("rvb", [127, 64], BF16)
        self.rv64row = at("rv64row", [1, 64], F32)
        self.onespadC = at("onespadC", [1, 127], F32)
        self.rk64c = at("rk64c", [128, 1], F32)
        self.rk0c = at("rk0c", [128, 1], F32)
        self.boc = at("boc", [128, KC], F32)
        self.cwc = at("cwc", [128, 3 * HC], F32)
        self.cbc = at("cbc", [128, HC], F32)
        self.swc = at("swc", [128, 4], F32)
        self.rndc_col = at("rndc_col", [128, 1], F32)
        self.eps_col = at("eps_col", [128, 1], F32)
        self.stat = at("stat", [128, 64], F32)
        self.shift32 = at("shift32", [32, 128], F32)
        self.x_tm = at("x_tm", [128, max(D, 1024)], F32)
        self.ctmp = at("ctmp", [64, S], BF16)
        self.sums3 = at("sums3", [67, S], F32)
        self.tmp_f = at("tmp_f", [128, S], F32)
        self.tmp_f2 = at("tmp_f2", [128, S], F32)
        self.wabs = at("wabs", [128, 64], F32)
        self.relt = at("relt", [65, 64], F32)
        self.rel2 = at("rel2", [64, 68], F32)
        peak_static = self._off
        pool = attn_pool_start
        pool_end = self._offs["poolpad"][1]
        blk2 = self._offs["sums3"][0]
        blk2_end = self._offs["tmp_f2"][1]
        self.x_f = at("x_f", [128, KC * S], F32, alias=pool)
        self.woT = at("woT", [128, KC * D], BF16, alias=pool + KC * S * 4)
        self.ctxc = at("ctxc", [128, 2 * 512], BF16,
                       alias=pool + KC * S * 4 + KC * D * 2)
        assert pool + KC * S * 4 + KC * D * 2 + 2048 <= pool_end
        self.nstat = at("nstat", [4, S], F32, alias=blk2)
        self.rstat_raw = at("rstat_raw", [2, S], F32, alias=blk2 + S * 4)
        self.rstat = at("rstat", [2, S], F32, alias=blk2 + 2 * S * 4)
        assert blk2 + 3 * S * 4 <= blk2_end
        self.rowB = at("rowB", [1, S], F32, alias=blk2)
        self.srow = at("srow", [1, S], F32, alias=blk2 + S * 4)
        self.arow = at("arow", [1, S], F32, alias=blk2 + 2 * S * 4)
        self.w1T = at("w1T", [128, KC * DH], BF16, alias=pool)
        nxt = pool + KC * DH * 2
        self.h2reg = at("h2reg", [128, HC * 128], F32, alias=nxt)
        nxt += HC * 128 * 4
        self.cbuf = at("cbuf", [128, 8 * 130], F32, alias=nxt)
        nxt += 8 * 130 * 4
        assert nxt <= pool_end, (nxt, pool_end)
        self.w2Tc = at("w2Tc", [128, 2 * D], BF16, alias=blk2)
        assert 2 * D * 2 <= S * 4
        if 2 * D * 2 + 2 * 128 * 2 + 128 * 4 <= S * 4:
            qb = blk2 + 2 * D * 2
        else:
            qb = self._offs["ctmp"][0]
            assert 2 * 128 * 2 + 128 * 4 <= S * 2
        self.xq2c = at("xq2c", [128, 2 * 128], BF16, alias=qb)
        self.a2sb = at("a2sb", [128, 128], F32, alias=qb + 2 * 128 * 2)
        total_arena = peak_static - self._base
        assert total_arena <= 200 * 1024, total_arena
        self.psA = nc.alloc_psum_tensor("psA", [128, 2048], F32)
        self.psB = nc.alloc_psum_tensor("psB", [128, 2048], F32)
        self.relay_buf = at("relay_buf", [1, 64], F32)
        self.b.relay_fn = lambda e: e.memset(self.relay_buf[0:1, 0:2], 0.0)
        self.b.relay_fns = [
            (lambda e, o=8 * k: e.memset(self.relay_buf[0:1, o:o + 2], 0.0))
            for k in range(8)]

    def dma(self, eng, out, in_, waits, slow=False, s2s=False):
        if s2s:
            waits = list(waits) + [self._tr_tok]
        if slow:
            t = self.b.emit(eng, waits, lambda e: e.dma_start(
                out=out, in_=in_, allow_slow_non_contiguous=True), inc=16)
        else:
            t = self.b.emit(eng, waits, lambda e: e.dma_start(out=out, in_=in_),
                            inc=16)
        if s2s:
            self._s2s_tok = t
        return t

    def dmaT(self, out, in_, waits):
        t = self.b.emit("sp", list(waits) + [self._s2s_tok, self._tr_tok],
                        lambda e: e.dma_start(out=out, in_=in_, transpose=True),
                        inc=16)
        self._tr_tok = t
        return t

    def mm(self, out, lhsT, rhs, waits, start, stop):
        return self.b.emit("pe", waits,
                           lambda e: e.matmul(out, lhsT, rhs, start=start, stop=stop),
                           inc=1)

    def tr(self, out, in_, waits, start=True, stop=True, f32=False):
        idn = self.identf if f32 else self.ident
        P = in_.shape[0]
        return self.b.emit("pe", waits,
                           lambda e: e.matmul(out, in_, idn[0:P, 0:P],
                                              start=start, stop=stop,
                                              is_transpose=True),
                           inc=1)

    def consts(self):
        def msets(e):
            e.memset(self.ident[:, :], 1.0)
            e.memset(self.identf[:, :], 1.0)
            e.memset(self.ones_col_f[:, :], 1.0)
            e.memset(self.ones_r1f[:, :], 1.0)
            e.memset(self.onespadB[:, 0:64], 0.0)
            e.memset(self.onespadB[:, 64:128], 1.0)
            e.memset(self.onespadA[:, 0:64], 1.0)
            e.memset(self.onespadA[:, 64:128], 0.0)
            e.memset(self.onespadC[:, 0:63], 0.0)
            e.memset(self.onespadC[:, 63:127], 1.0)
            e.memset(self.zeros_b[:, :], 0.0)
            e.memset(self.zcol67[:, :], 0.0)
            e.memset(self.zcol128[:, :], 0.0)
            e.memset(self.mask_hi[:, :], 1.0)
            e.memset(self.mask_lo[:, :], 1.0)
            e.memset(self.mask_bb0[:, :], 1.0)
            e.memset(self.mask_bbL[:, :], 1.0)
            e.memset(self.wskew[:, :], 0.0)
            e.memset(self.au_band[:, :], 0.0)
            e.memset(self.bb[:, :], 0.0)
            e.memset(self.rv4[:, :], 0.0)
            e.memset(self.sel2[:, :], 1.0)
            e.memset(self.stat[:, :], 0.0)
            e.memset(self.wabs[:, 32:48], 127.0)
            e.memset(self.rndc_col[:, :], RND_C)
            e.memset(self.eps_col[:, :], 1e-5)
            for q in range(4):
                o = q * BAND_L
                e.memset(self.fband[:, o:o + 97], 0.0)
                e.memset(self.fband[:, o + 160:o + 224], 1.0)
                e.memset(self.fband[:, o + 224:o + BAND_L], 0.0)
            vt5 = self.v_tm[:, :].rearrange("p (t h v c) -> p t h v c", h=2, v=2, c=67)
            e.memset(vt5[:, :, :, 0:1, 64:66], 1.0)
            e.memset(vt5[:, :, :, 0:1, 66:67], 0.0)
            e.memset(vt5[:, :, :, 1:2, 64:65], 1.0)
            e.memset(vt5[:, :, :, 1:2, 65:66], 0.0)
            return e.memset(vt5[:, :, :, 1:2, 66:67], 1.0)
        t_m = self.b.emit("dve", [], msets)

        def aff(e):
            e.affine_select(self.ident[:, :], self.ident[:, :], [[-1, 128]],
                            ALU.is_equal, 0.0, base=0, channel_multiplier=1)
            e.affine_select(self.identf[:, :], self.identf[:, :], [[-1, 128]],
                            ALU.is_equal, 0.0, base=0, channel_multiplier=1)
            e.affine_select(self.mask_hi[:, :], self.mask_hi[:, :], [[-1, 128]],
                            ALU.is_ge, 0.0, base=0, channel_multiplier=1)
            e.affine_select(self.mask_lo[:, :], self.mask_lo[:, :], [[1, 128]],
                            ALU.is_ge, 0.0, base=0, channel_multiplier=-1)
            e.affine_select(self.mask_bb0[:, :], self.mask_bb0[:, :], [[1, 127]],
                            ALU.is_ge, 0.0, base=-31, channel_multiplier=1)
            e.affine_select(self.sel2[:, :], self.sel2[:, :], [[1, 128]],
                            ALU.is_ge, 0.0, base=0, channel_multiplier=-64)
            return e.affine_select(self.mask_bbL[:, :], self.mask_bbL[:, :],
                                   [[-1, 127]], ALU.is_ge, 0.0, base=158,
                                   channel_multiplier=-1)
        t_a0 = self.b.emit("gpc", [t_m], aff)
        self.t_const = self.b.emit("gpc", [t_a0], lambda e: e.affine_select(
            self.sel2[:, :], self.sel2[:, :], [[-1, 128]],
            ALU.is_ge, 0.0, base=63, channel_multiplier=64))

    def small_inputs(self):
        b = self.b
        t1 = self.dma("sp", self.relt[:, :], self.rk_in[:, :], [self.t_const])
        tt = self.tr(self.psA[0:64, 0:65], self.relt[:, 0:64], [t1], f32=True)
        tev = b.emit("dve", [tt], lambda e: e.tensor_copy(self.rel2[:, 0:65],
                                                          self.psA[0:64, 0:65]))
        trkq = b.emit("dve", [tev], lambda e: e.tensor_scalar(
            self.rkq[0:64, :], self.rel2[:, 0:65], self.rel2[:, 64:65], None,
            ALU.subtract))
        t64 = b.emit("dve", [tev], lambda e: e.tensor_copy(self.rk64c[0:64, :],
                                                           self.rel2[:, 64:65]))
        t0 = b.emit("dve", [tev], lambda e: e.tensor_copy(self.rk0c[0:64, :],
                                                          self.rel2[:, 0:1]))
        tds = [self.dma("gp", self.rkq[64:128, :], self.rkq[0:64, :], [trkq], s2s=True),
               self.dma("gp", self.rk64c[64:128, :], self.rk64c[0:64, :], [t64], s2s=True),
               self.dma("gp", self.rk0c[64:128, :], self.rk0c[0:64, :], [t0], s2s=True)]
        t64r = self.dma("sp", self.rv64row[:, :], self.rv_in[64:65, 0:64],
                        [self.t_const])
        tmm64 = self.mm(self.psA[0:127, 0:64], self.onespadC[:, :],
                        self.rv64row[:, :], [t64r, tev], start=True, stop=True)
        tev64 = b.emit("dve", [tmm64], lambda e: e.tensor_copy(
            self.rvb[0:127, :], self.psA[0:127, 0:64]))
        trvb = self.dma("gp", self.rvb[0:63, :], self.rv_in[1:64, 0:64],
                        [tev64])
        tr4 = [self.dma("gp", self.rv4[0:1, 0:64], self.rv_in[64:65, 0:64],
                        [self.t_const]),
               self.dma("gp", self.rv4[1:2, 0:64], self.rv_in[0:1, 0:64],
                        [self.t_const]),
               self.dma("gp", self.rv4[2:3, 64:128], self.rv_in[64:65, 0:64],
                        [self.t_const]),
               self.dma("gp", self.rv4[3:4, 64:128], self.rv_in[0:1, 0:64],
                        [self.t_const])]
        tb = [self.dma("sp", self.boc[:, k:k + 1],
                       bass.AP(self.bo_in, k * 128, [[1, 128], [128, 1]]),
                       [self.t_const], slow=True) for k in range(self.KC)]
        tcw = [self.dma("sp", self.cwc[:, 3 * h:3 * h + 3],
                        bass.AP(self.cw_in, h * 384, [[3, 128], [1, 3]]),
                        [self.t_const]) for h in range(self.HC)]
        tcb = [self.dma("sp", self.cbc[:, h:h + 1],
                        bass.AP(self.cb_in, h * 128, [[1, 128], [128, 1]]),
                        [self.t_const], slow=True) for h in range(self.HC)]
        tpad = self.dma("gp", bass.AP(self.auband_d, 0, [[1, 128]]),
                        self.zeros_b[0:1, 0:128], [self.t_const])
        self.t_small = tds + [trvb, tpad] + tb + tcw + tcb + tr4
        self._psA_init = tev64

    def load_x(self, to_xf=False, dep=()):
        b, S, KC, NT = self.b, self.S, self.KC, self.NT
        prev = [self.t_const, self._psA_init] + list(dep)
        for tt in range(NT):
            tl = self.dma("sp", self.x_tm[:, 0:self.D],
                          self.x_in[tt * 128:(tt + 1) * 128, :], prev)
            evs = []
            for kc in range(KC):
                ps = self.psA if kc < 4 else self.psB
                bank = ps[:, (kc % 4) * 512:(kc % 4) * 512 + 128]
                ttr = self.tr(bank, self.x_tm[:, kc * 128:(kc + 1) * 128],
                              [tl] + prev, start=True, stop=True, f32=True)
                if to_xf:
                    dst = self.x_f[:, kc * S + tt * 128: kc * S + tt * 128 + 128]
                    evs.append(b.emit("dve", [ttr],
                                      lambda e, d=dst, s=bank: e.tensor_copy(d, s)))
                else:
                    dst = self.xbf[:, kc * S + tt * 128: kc * S + tt * 128 + 128]
                    evs.append(b.emit("dve", [ttr],
                                      lambda e, d=dst, s=bank: e.tensor_copy(d, s)))
            prev = evs[-2:]
        if to_xf:
            self.t_xf = prev
        else:
            self.t_x = prev

    def weights_qkvo(self):
        b, D, KC = self.b, self.D, self.KC
        self.o_pend = None
        self.t_wT = {}
        prev = [None, None]
        for nm, win, stg, stgT in (("q", self.wq_in, self.stg_q, self.stg_qt),
                                   ("k", self.wk_in, self.stg_k, self.stg_kt),
                                   ("v", self.wv_in, self.stg_v, self.stg_vt),
                                   ("o", self.wo_in, self.stg_o, None)):
            tst = []
            per = min(2048, D)
            ngrp = (D * D) // (128 * per)
            for gi in range(ngrp):
                pp = gi % 2
                src = win[:, :].rearrange("r c -> (r c)")[
                    gi * 128 * per:(gi + 1) * 128 * per].rearrange(
                    "(p n) -> p n", p=128)
                tl = self.dma("sp", self.wstg[:, pp * 2048: pp * 2048 + per], src,
                              [self.t_const, prev[pp]])
                tc = b.emit("dve", [tl], lambda e, pp=pp, per=per: e.tensor_copy(
                    self.wbf16[:, pp * 2048: pp * 2048 + per],
                    self.wstg[:, pp * 2048: pp * 2048 + per]))
                dst = stg[:, :].rearrange("r c -> (r c)")[
                    gi * 128 * per:(gi + 1) * 128 * per].rearrange(
                    "(p n) -> p n", p=128)
                ts_ = self.dma("sp", dst,
                               self.wbf16[:, pp * 2048: pp * 2048 + per], [tc])
                prev[pp] = ts_
                tst.append(ts_)
            if nm == "o":
                self.o_pend = tst
                continue
            toks = []
            prev_wr = [prev[0], prev[1]]
            NP = self.NP
            for kc in range(KC):
                pp = kc % 2
                tk = self.dmaT(self.wbf16[:, pp * 2048: pp * 2048 + D],
                               stg[0:D, kc * 128:(kc + 1) * 128],
                               tst + [prev_wr[pp]])
                tw = self.dma("sp",
                              bass.AP(stgT, kc * 128,
                                      [[KC * 128, 128], [128 * D, NP], [1, 128]]),
                              self.wbf16[:, pp * 2048: pp * 2048 + D].rearrange(
                                  "f (p c) -> f p c", c=128),
                              [tk])
                prev_wr[pp] = tw
                toks.append(tw)
            self.t_wT[nm] = toks
            for pp in range(2):
                if prev_wr[pp] is not prev[pp]:
                    prev[pp] = prev_wr[pp]
        self._wstg_war = list(prev)

    def weights_w12(self):
        b, D, DH, KC, HC = self.b, self.D, self.DH, self.KC, self.HC

        def groups(rows, cols):
            total = rows * cols
            per_n = min(2048, total // 128)
            return total // (128 * per_n), per_n

        toks_abs = []
        prev = list(self._wstg_war)
        g = 0
        self.ng1 = groups(DH, D)[0]
        for win, rows, cols in ((self.w1_in, DH, D), (self.w2_in, D, DH)):
            ngrp, per_n = groups(rows, cols)
            for gi in range(ngrp):
                pp = gi % 2
                src = win[:, :].rearrange("r c -> (r c)")[
                    gi * 128 * per_n:(gi + 1) * 128 * per_n].rearrange(
                    "(p n) -> p n", p=128)
                tl = self.dma("sp", self.wstg[:, pp * 2048: pp * 2048 + per_n], src,
                              [self.t_const, prev[pp]])
                ta = b.emit("dve", [tl], lambda e, pp=pp, g=g, per_n=per_n:
                            e.reduce_sum(self.wabs[:, g:g + 1],
                                         self.wstg[:, pp * 2048: pp * 2048 + per_n],
                                         axis=mybir.AxisListType.X,
                                         apply_absolute_value=True))
                prev[pp] = ta
                toks_abs.append(ta)
                g += 1
        ngtot = g
        tmm = self.mm(self.psA[0:1, 0:ngtot], self.ones_col_f[:, :],
                      self.wabs[:, 0:ngtot], toks_abs + self.t_x,
                      start=True, stop=True)

        st = self.stat[0:1, :]
        ts1 = b.emit("dve", [tmm], lambda e: (
            e.reduce_sum(st[:, 0:1], self.psA[0:1, 0:self.ng1],
                         axis=mybir.AxisListType.X),
            e.reduce_sum(st[:, 1:2], self.psA[0:1, self.ng1:ngtot],
                         axis=mybir.AxisListType.X))[-1])
        ts2 = b.emit("dve", [ts1], lambda e: (
            e.tensor_scalar(st[:, 2:3], st[:, 0:1],
                            1.0 / (DH * D), 1e-5, ALU.mult, ALU.max),
            e.tensor_scalar(st[:, 3:4], st[:, 1:2],
                            1.0 / (DH * D), 1e-5, ALU.mult, ALU.max))[-1])
        tdv = b.emit("dve", [ts2], lambda e: e.reciprocal(
            st[:, 4:6], st[:, 2:4]))
        tsc = b.emit("dve", [tdv], lambda e: e.tensor_scalar(
            st[:, 6:8], st[:, 2:4], 1.0 / 127.0, None, ALU.mult))

        def reorder(e):
            e.tensor_copy(st[:, 8:9], st[:, 4:5])
            e.tensor_copy(st[:, 9:10], st[:, 6:7])
            e.tensor_copy(st[:, 10:11], st[:, 5:6])
            return e.tensor_copy(st[:, 11:12], st[:, 7:8])
        tro = b.emit("dve", [tsc], reorder)
        tbc = self.mm(self.psB[0:128, 0:4], self.ones_r1f[:, :], st[:, 8:12],
                      [tro], start=True, stop=True)
        tswc = b.emit("dve", [tbc], lambda e: e.tensor_copy(self.swc[:, :],
                                                            self.psB[0:128, 0:4]))
        self.t_swc = tswc
        prev = [None, None]
        tq_out = []
        for wi, (win, stg, rows, cols, scol) in enumerate(
                ((self.w1_in, self.stg_w1, DH, D, 0),
                 (self.w2_in, self.stg_w2, D, DH, 2))):
            ngrp, per_n = groups(rows, cols)
            for gi in range(ngrp):
                pp = gi % 2
                sl = slice(pp * 2048, pp * 2048 + per_n)
                src = win[:, :].rearrange("r c -> (r c)")[
                    gi * 128 * per_n:(gi + 1) * 128 * per_n].rearrange(
                    "(p n) -> p n", p=128)
                tl = self.dma("sp", self.wstg[:, sl], src, [tswc, prev[pp]])
                ta0 = b.emit("dve", [tl, tswc], lambda e, sl=sl, scol=scol:
                             e.tensor_scalar(self.wstg[:, sl], self.wstg[:, sl],
                                             self.swc[:, scol:scol + 1], None,
                                             ALU.mult))
                ta = b.emit("dve", [ta0], lambda e, sl=sl: e.tensor_scalar(
                    self.wstg[:, sl], self.wstg[:, sl], RND_C, None, ALU.add))
                t1 = b.emit("dve", [ta], lambda e, sl=sl: e.tensor_scalar(
                    self.wstg[:, sl], self.wstg[:, sl], RND_C, -1.0,
                    ALU.subtract, ALU.max))
                t2 = b.emit("dve", [t1], lambda e, sl=sl: e.tensor_scalar(
                    self.wbf16[:, sl], self.wstg[:, sl], 1.0, None, ALU.min))
                dst = stg[:, :].rearrange("r c -> (r c)")[
                    gi * 128 * per_n:(gi + 1) * 128 * per_n].rearrange(
                    "(p n) -> p n", p=128)
                ts_ = self.dma("sp", dst, self.wbf16[:, sl], [t2])
                prev[pp] = ts_
                tq_out.append(ts_)
        self.t_wq_done = tq_out
        self.t_w2t = []
        prev_wr = None
        for hc in range(HC):
            pp = hc % 2
            tk = self.dmaT(self.wbf16[:, pp * 2048: pp * 2048 + D],
                           self.stg_w2[0:D, hc * 128:(hc + 1) * 128],
                           tq_out + [prev_wr])
            tw = self.dma("sp", self.stg_w2t[hc * 128:(hc + 1) * 128, :],
                          self.wbf16[:, pp * 2048: pp * 2048 + D], [tk])
            prev_wr = tw
            self.t_w2t.append(tw)
        self.t_w1t = []
        chunk = min(2048, DH)
        nch = DH // chunk
        prev_wr = [prev_wr, prev_wr]
        ci = 0
        for kc in range(KC):
            for h in range(nch):
                pp = ci % 2
                ci += 1
                tk = self.dmaT(self.wbf16[:, pp * 2048: pp * 2048 + chunk],
                               self.stg_w1[h * chunk:(h + 1) * chunk,
                                           kc * 128:(kc + 1) * 128],
                               tq_out + [prev_wr[pp]])
                tw = self.dma("sp",
                              self.stg_w1t[kc * 128:(kc + 1) * 128,
                                           h * chunk:(h + 1) * chunk],
                              self.wbf16[:, pp * 2048: pp * 2048 + chunk], [tk])
                prev_wr[pp] = tw
                self.t_w1t.append(tw)

    def _proj_pair(self, p):
        b, S, D, KC, NT = self.b, self.S, self.D, self.KC, self.NT
        pg = p % 2
        wsl = slice(pg * KC * 128, pg * KC * 128 + KC * 128)
        tw = []
        for buf, stgT in ((self.wqp, self.stg_qt), (self.wkp, self.stg_kt),
                          (self.wvp, self.stg_vt)):
            src = bass.AP(stgT, p * 128 * D, [[KC * 128, 128], [1, KC * 128]])
            tw.append(self.dma("sp", buf[:, wsl], src,
                      self.t_wT["q" if buf is self.wqp else
                                ("k" if buf is self.wkp else "v")] +
                      [self._wslice_war[pg]]))
        tq, tk, tv = [], [], []
        bank_war = {}
        for which, wT in (("q", self.wqp), ("k", self.wkp)):
            for nb in range(self.NB):
                bank = self.psA[:, nb * 512:(nb + 1) * 512]
                last = None
                for kc in range(KC):
                    last = self.mm(bank,
                                   wT[:, pg * KC * 128 + kc * 128:
                                      pg * KC * 128 + kc * 128 + 128],
                                   self.xbf[:, kc * S + nb * 512: kc * S + nb * 512 + 512],
                                   tw + self.t_x + [self._pair_war, self._psA_war,
                                                    bank_war.get(nb)] +
                                   self._dbg_pair_wars,
                                   start=(kc == 0), stop=(kc == KC - 1))
                if which == "q":
                    te = b.emit("dve", [last], lambda e, nb=nb, bank=bank:
                                e.tensor_copy(self.q_pair[:, nb * 512:(nb + 1) * 512],
                                              bank))
                    tq.append(te)
                    bank_war[nb] = te
                else:
                    te1 = b.emit("dve", [last] + self.t_small,
                                 lambda e, nb=nb, bank=bank: e.tensor_scalar(
                                     self.khi[:, nb * 512:(nb + 1) * 512], bank,
                                     self.rk64c[:, :], None, ALU.add))
                    te2 = b.emit("dve", [te1], lambda e, nb=nb, bank=bank:
                                 e.tensor_scalar(self.klo[:, nb * 512:(nb + 1) * 512],
                                                 bank, self.rk0c[:, :], None, ALU.add))
                    tk += [te1, te2]
                    bank_war[nb] = te2
        self._psA_war = (tq + tk)[-1]
        for tt in range(NT):
            slot = self.psB[:, (tt % 4) * 512:(tt % 4) * 512 + 128]
            last = None
            for kc in range(KC):
                last = self.mm(slot,
                               self.xbf[:, kc * S + tt * 128: kc * S + tt * 128 + 128],
                               self.wvp[:, pg * KC * 128 + kc * 128:
                                         pg * KC * 128 + kc * 128 + 128],
                               tw + self.t_x +
                               [self._pair_war, self._vwar[tt % 4], self.t_swc] +
                               self._dbg_pair_wars,
                               start=(kc == 0), stop=(kc == KC - 1))
            evs = []
            for hp in range(2):
                for var in range(2):
                    base = tt * self.VW + hp * 134 + var * 67
                    evs.append(b.emit("dve", [last] + [self.t_const],
                                      lambda e, base=base, hp=hp, slot=slot:
                                      e.tensor_copy(self.v_tm[:, base:base + 64],
                                                    slot[:, hp * 64:hp * 64 + 64])))
            self._vwar[tt % 4] = evs[-1]
            tv += evs
        self._wslice_war[pg] = (tq + tk + tv)[-1]
        return tq, tk, tv

    def _band_head(self, p, hp, dep, t_zero):
        b, S, NT = self.b, self.S, self.NT
        hs = slice(hp * 64, hp * 64 + 64)
        toks_mm = []
        for qtg in range((NT + 3) // 4):
            qts = [qtg * 4 + i for i in range(4) if qtg * 4 + i < NT]
            nqt = len(qts)
            pm = [self.mm(self.psA[:, i * 512: i * 512 + 65],
                          self.q_pair[hs, qt * 128: qt * 128 + 128],
                          self.rkq[hs, 0:65],
                          dep + [self._psA_war], start=True, stop=True)
                  for i, qt in enumerate(qts)]
            psrc = self.psA[:, 0:nqt * 512].rearrange(
                "x (q c) -> x q c", c=512)[:, :, 1:64]
            fdst = self.fband[:, 0:nqt * BAND_L].rearrange(
                "x (q c) -> x q c", c=BAND_L)[:, :, 97:160]
            texp = b.emit("act", pm + [self._fband_war],
                          lambda e, psrc=psrc, fdst=fdst: e.activation(
                              fdst, psrc, AF.Exp, scale=self.inv8))
            self._psA_war = texp
            slot0 = (hp * 4) * 128 * BAND_L
            wdst = bass.AP(self.band_f, slot0,
                           [[BAND_L, 128], [128 * BAND_L, nqt], [1, BAND_L]])
            wsrc = bass.AP(self.fband, 0,
                           [[4 * BAND_L, 128], [BAND_L, nqt], [1, BAND_L]])
            twr = self.dma("gp", wdst, wsrc, [texp, self._wskew_war])
            self._fband_war = twr
            rtoks = []
            reg = [i for i, qt in enumerate(qts) if qt != 0]
            if reg:
                i0 = reg[0]
                rsrc = bass.AP(self.band_f, slot0 + i0 * 128 * BAND_L,
                               [[BAND_L - 1, 128], [128 * BAND_L, len(reg)],
                                [1, BAND_L]])
                rdst = bass.AP(self.wskew, i0 * BAND_L,
                               [[4 * BAND_L, 128], [BAND_L, len(reg)], [1, BAND_L]])
                rtoks.append(self.dma("gp", rdst, rsrc, [twr]))
            if 0 in qts:
                rsrc = bass.AP(self.band_f, slot0 + 128,
                               [[BAND_L - 1, 128], [1, 256]])
                rtoks.append(self.dma("gp", self.wskew[:, 0:256], rsrc, [twr]))
            smm, exps = [], []
            ab0 = hp * 4 * BAND_L
            for i, qt in enumerate(qts):
                qb = qt * 128
                wb = max(0, qb - 128)
                wd = min(S, qb + 256) - wb
                smm.append(self.mm(self.psA[:, i * 512: i * 512 + wd],
                                   self.q_pair[hs, qb:qb + 128],
                                   self.khi[hs, wb:wb + wd],
                                   dep + [self._psA_war], start=True, stop=True))
                exps.append(b.emit("act", [smm[i], self._auband_war],
                                   lambda e, i=i, wd=wd, ab0=ab0: e.activation(
                    self.au_band[:, ab0 + i * BAND_L: ab0 + i * BAND_L + wd],
                    self.psA[:, i * 512: i * 512 + wd], AF.Exp, scale=self.inv8)))
            self._psA_war = exps[-1]
            tmul = b.emit("dve", exps + rtoks,
                          lambda e, ab0=ab0, n=nqt * BAND_L: e.tensor_mul(
                              self.au_band[:, ab0:ab0 + n],
                              self.au_band[:, ab0:ab0 + n], self.wskew[:, 0:n]))
            self._wskew_war = tmul
            for i, qt in enumerate(qts):
                qb = qt * 128
                wb = max(0, qb - 128)
                wd = min(S, qb + 256) - wb
                nblk = wd // 128
                trs = [self.tr(self.psA[:, i * 512 + k * 64: i * 512 + k * 64 + 64
                                        ].bitcast(BF16),
                               self.au_band[:, ab0 + i * BAND_L + k * 128:
                                            ab0 + i * BAND_L + k * 128 + 128],
                               [tmul, self._abT_war, self._psA_war],
                               start=(k == 0), stop=(k == nblk - 1))
                       for k in range(nblk)]
                ev = b.emit("dve", trs, lambda e, i=i, wd=wd: e.tensor_copy(
                    self.abT[:, i * BAND_L: i * BAND_L + wd],
                    self.psA[:, i * 512: i * 512 + wd // 2].bitcast(BF16)))
                self._psA_war = ev
                for k in range(nblk):
                    vt = wb // 128 + k
                    toks_mm.append(self.mm(
                        self.psB[0:65, qb:qb + 128],
                        self.v_tm[:, vt * self.VW + hp * 134:
                                  vt * self.VW + hp * 134 + 65],
                        self.abT[:, i * BAND_L + k * 128: i * BAND_L + k * 128 + 128],
                        [ev, t_zero], start=False, stop=False))
                self._abT_war = toks_mm[-1]
            ring = hp + 2 * (qtg % 2)
            rbase = 128 + ring * 128 * 4 * BAND_L
            tabw = self.dma("gp",
                            bass.AP(self.auband_d, rbase,
                                    [[4 * BAND_L, 128], [1, nqt * BAND_L]]),
                            self.au_band[:, ab0:ab0 + nqt * BAND_L],
                            [tmul, self._bb_ring_war[ring]])
            self._auband_war = tabw
            bbts = []
            if reg:
                i0 = reg[0]
                src = bass.AP(self.auband_d, rbase + i0 * BAND_L + 97,
                              [[4 * BAND_L + 1, 128], [BAND_L, len(reg)], [1, 127]])
                dstb = bass.AP(self.bb, hp * 512 + i0 * 128,
                               [[2 * 4 * 128, 128], [128, len(reg)], [1, 127]])
                bbts.append(self.dma("gp", dstb, src, [tabw, self._bb_war]))
            if 0 in qts:
                src = bass.AP(self.auband_d, rbase - 31,
                              [[4 * BAND_L + 1, 128], [1, 127]])
                bbts.append(self.dma("gp", self.bb[:, hp * 512: hp * 512 + 127],
                                     src, [tabw, self._bb_war]))
            self._bb_ring_war[ring] = bbts[-1]
            tmk = list(bbts)
            if 0 in qts:
                tmk.append(b.emit("dve", bbts, lambda e: e.tensor_mul(
                    self.bb[:, hp * 512: hp * 512 + 127],
                    self.bb[:, hp * 512: hp * 512 + 127], self.mask_bb0[:, 0:127])))
            if (NT - 1) in qts:
                iL = qts.index(NT - 1)
                sl = slice(hp * 512 + iL * 128, hp * 512 + iL * 128 + 127)
                tmk.append(b.emit("dve", bbts, lambda e, sl=sl: e.tensor_mul(
                    self.bb[:, sl], self.bb[:, sl], self.mask_bbL[:, 0:127])))
            btr = [self.tr(self.psA[0:127, i * 512: i * 512 + 64].bitcast(BF16),
                           self.bb[:, hp * 512 + i * 128: hp * 512 + i * 128 + 127],
                           tmk + [self._psA_war], start=True, stop=True)
                   for i, qt in enumerate(qts)]
            psrc2 = self.psA[0:127, 0:nqt * 512].bitcast(BF16).rearrange(
                "x (q c) -> x q c", c=1024)[:, :, 0:128]
            bev = b.emit("dve", btr, lambda e, psrc2=psrc2, nqt=nqt: e.tensor_copy(
                self.battn[:, 0:nqt * 128], psrc2))
            self._psA_war = bev
            self._bb_war = bev
            toks_mm.append(self.mm(self.psB[0:64, qtg * 512: qtg * 512 + nqt * 128],
                                   self.rvb[:, :], self.battn[:, 0:nqt * 128],
                                   [bev, t_zero, self._battn_war],
                                   start=False, stop=False))
            self._battn_war = toks_mm[-1]
        return toks_mm

    def _main_head(self, p, hp, dep, t_zero):
        b, S, NT = self.b, self.S, self.NT
        hs = slice(hp * 64, hp * 64 + 64)
        toks_av = []
        exp_war = [None, None]
        for kt in range(NT):
            a = kt * 128
            mm_toks = []
            for nb in range(self.NB):
                c0, c1 = nb * 512, nb * 512 + 512
                hi_end = min(max(a + 32, c0), c1)
                pieces = []
                if hi_end > c0:
                    pieces.append((self.khi, c0, hi_end))
                if c1 > hi_end:
                    pieces.append((self.klo, hi_end, c1))
                for pi, (kbuf, s0, s1) in enumerate(pieces):
                    mm_toks.append(self.mm(self.psA[:, s0:s1],
                                           kbuf[hs, a:a + 128],
                                           self.q_pair[hs, s0:s1],
                                           dep + [self._psA_war],
                                           start=(pi == 0),
                                           stop=(pi == len(pieces) - 1)))
            pg = kt % 2
            au0 = pg * S
            texp = b.emit("act", mm_toks + [exp_war[pg]],
                          lambda e, au0=au0: e.activation(
                              self.au[:, au0:au0 + S], self.psA[:, 0:S],
                              AF.Exp, scale=self.inv8))
            self._psA_war = texp
            tmks = []
            hs0 = max(0, a - 96)
            if a + 32 > hs0:
                moff = hs0 - (a - 96)
                w = a + 32 - hs0
                tmks.append(b.emit("dve", [texp],
                                   lambda e, hs0=hs0, moff=moff, w=w, au0=au0:
                                   e.tensor_mul(
                    self.au[:, au0 + hs0: au0 + hs0 + w],
                    self.au[:, au0 + hs0: au0 + hs0 + w],
                    self.mask_hi[:, moff:moff + w])))
            ls1 = min(S, a + 160)
            if ls1 > a + 32:
                w = ls1 - (a + 32)
                tmks.append(b.emit("dve", [texp], lambda e, a=a, w=w, au0=au0:
                                   e.tensor_mul(
                    self.au[:, au0 + a + 32: au0 + a + 32 + w],
                    self.au[:, au0 + a + 32: au0 + a + 32 + w],
                    self.mask_lo[:, 0:w])))
            kt_av = []
            for nb in range(self.NB):
                c0, c1 = nb * 512, nb * 512 + 512
                hi_end = min(max(a + 32, c0), c1)
                segs = []
                if hi_end > c0:
                    segs.append((0, c0, hi_end))
                if c1 > hi_end:
                    segs.append((1, hi_end, c1))
                for var, s0, s1 in segs:
                    kt_av.append(self.mm(
                        self.psB[0:67, s0:s1],
                        self.v_tm[:, kt * self.VW + hp * 134 + var * 67:
                                  kt * self.VW + hp * 134 + var * 67 + 67],
                        self.au[:, au0 + s0: au0 + s1],
                        tmks + [texp, t_zero], start=False, stop=False))
            exp_war[pg] = kt_av[-1]
            toks_av += kt_av
        return toks_av

    def attention(self):
        b, S, NT, NP = self.b, self.S, self.NT, self.NP
        self._pair_war = None
        self._dbg_pair_wars = []
        self._wslice_war = [None, None]
        self._vwar = [None] * 4
        self._psA_war = None
        self._abT_war = None
        self._bb_war = None
        self._battn_war = None
        self._auband_war = None
        self._fband_war = None
        self._wskew_war = None
        self._bb_ring_war = [None] * 4
        self._ctmp_war = None
        self._ctx_war = None
        self._sums_war = None
        self.t_ctx_dma = []
        self.t_sums_dma = []
        for p in range(NP):
            tq, tk, tv = self._proj_pair(p)
            dep = tq + tk
            for hp in range(2):
                h = 2 * p + hp
                zts = []
                for nb in range(self.NB):
                    zts.append(self.mm(self.psB[0:67, nb * 512:(nb + 1) * 512],
                                       self.zcol67[:, :], self.zeros_b[:, :],
                                       tv + [self._ctx_war],
                                       start=True, stop=False))
                t_zero = zts[-1]
                band_toks = self._band_head(p, hp, dep, t_zero)
                main_toks = self._main_head(p, hp, dep, t_zero)
                cls = [self.mm(self.psB[0:67, nb * 512:(nb + 1) * 512],
                               self.zcol67[:, :], self.zeros_b[:, :],
                               band_toks[-1:] + main_toks[-1:] + band_toks[-3:],
                               start=False, stop=True)
                       for nb in range(self.NB)]
                tse = b.emit("dve", cls + main_toks[-2:] + band_toks[-2:] +
                             [self._sums_war],
                             lambda e: e.tensor_copy(self.sums3[64:67, 0:S],
                                                     self.psB[64:67, 0:S]))
                tsd = self.dma("gp", self.sums_d[3 * h:3 * h + 3, :],
                               self.sums3[64:67, 0:S], [tse])
                self._sums_war = tsd
                self.t_sums_dma.append(tsd)
                tce = b.emit("dve", [tse, self._ctmp_war], lambda e: e.tensor_copy(
                    self.ctmp[:, 0:S], self.psB[0:64, 0:S]))
                tcd = self.dma("gp",
                               self.ctx_d[p * 128 + hp * 64: p * 128 + hp * 64 + 64, :],
                               self.ctmp[:, 0:S], [tce])
                self._ctmp_war = tcd
                self.t_ctx_dma.append(tcd)
                self._ctx_war = tce
            self._pair_war = self._ctx_war
            if p == 0 and self.debug:
                tps = [self.tap("qp", self.q_pair[:, :], tq),
                       self.tap("khi", self.khi[:, :], tk),
                       self.tap("vtm", self.v_tm[:, :], tv),
                       self.tap("au", self.au[:, :], [self._ctx_war]),
                       self.tap("sums3", self.sums3[64:67, :], [self._ctx_war])]
                self._dbg_pair_wars = tps

    def post_attention(self):
        b, S, D, KC, NP = self.b, self.S, self.D, self.KC, self.NP
        _u = getattr(self, "p4_upto", 3)
        if _u == -1:
            dep0 = []
        elif _u == -2:
            dep0 = self.t_ctx_dma + self.t_sums_dma
        elif _u == -3:
            dep0 = self.t_w2t + self.t_w1t
        elif _u == -4:
            dep0 = [self._psA_war]
        elif _u == -5:
            dep0 = (self.t_ctx_dma + self.t_sums_dma +
                    self.t_w2t + self.t_w1t)
        elif _u == -6:
            dep0 = [self._psA_war] + self.t_ctx_dma + self.t_sums_dma
        elif _u == -7:
            dep0 = [self._psA_war] + self.t_w2t + self.t_w1t
        else:
            dep0 = ([self._psA_war] + self.t_ctx_dma +
                    self.t_sums_dma + self.t_w2t + self.t_w1t)
        self.load_x(to_xf=True, dep=dep0)
        self._psA_war = self.t_xf[-1]
        self._ctxc_war = [self.t_w1t[-1], self.t_w2t[-1]]
        if getattr(self, "p4_upto", 3) < 1:
            self.t_resid = list(self.t_xf)
            self.t_comb = []
            return
        tprev = None
        nstat_war = None
        rraw_war = None
        comb = []
        for p in range(NP):
            tg = [self.dma("sp", self.nstat[0:2, :],
                           bass.AP(self.sums_d, (6 * p + 1) * S, [[S, 2], [1, S]]),
                           self.t_sums_dma + [nstat_war]),
                  self.dma("sp", self.nstat[2:4, :],
                           bass.AP(self.sums_d, (6 * p + 4) * S, [[S, 2], [1, S]]),
                           self.t_sums_dma + [nstat_war]),
                  self.dma("sp", self.rstat_raw[0:2, :],
                           bass.AP(self.sums_d, 6 * p * S, [[3 * S, 2], [1, S]]),
                           self.t_sums_dma + [rraw_war])]
            trc = b.emit("dve", tg + [tprev], lambda e: e.reciprocal(
                self.rstat[0:2, :], self.rstat_raw[0:2, :]))
            rraw_war = trc
            tb_ = None
            for nb in range(self.NB):
                c0 = nb * 512
                f0 = self.mm(self.psA[0:128, c0:c0 + 512], self.sel2[:, :],
                             self.rstat[:, c0:c0 + 512],
                             [trc, tprev, self._psA_war], start=True, stop=True)
                rr = self.mm(self.psB[0:128, c0:c0 + 512], self.rv4[:, :],
                             self.nstat[0:4, c0:c0 + 512],
                             tg + [tprev, self._ctx_war, self._psA_war],
                             start=True, stop=True)
                nstat_war = rr
                pp = nb % 2
                tld = self.dma("sp", self.ctxc[:, pp * 512:(pp + 1) * 512],
                               self.ctx_d[p * 128:(p + 1) * 128, c0:c0 + 512],
                               self.t_ctx_dma + [self._ctxc_war[pp]])
                ta = b.emit("dve", [rr, tld, tb_], lambda e, pp=pp, c0=c0:
                            e.tensor_add(self.x_tm[:, 0:512],
                                         self.ctxc[:, pp * 512:(pp + 1) * 512],
                                         self.psB[0:128, c0:c0 + 512]))
                tb_ = b.emit("dve", [ta, f0], lambda e, pp=pp, c0=c0:
                             e.tensor_mul(self.ctxc[:, pp * 512:(pp + 1) * 512],
                                          self.x_tm[:, 0:512],
                                          self.psA[0:128, c0:c0 + 512]))
                tst = self.dma("sp", self.ctxn_d[p * 128:(p + 1) * 128, c0:c0 + 512],
                               self.ctxc[:, pp * 512:(pp + 1) * 512], [tb_])
                self._ctxc_war[pp] = tst
                comb.append(tst)
            tprev = comb[-1]
        self.t_comb = comb
        if getattr(self, "p4_upto", 3) < 2:
            self.t_resid = list(comb)
            return
        two = [self.dmaT(self.woT[:, kc * D:(kc + 1) * D],
                         self.stg_o[0:D, kc * 128:(kc + 1) * 128],
                         self.o_pend + self.t_w2t + self.t_w1t)
               for kc in range(KC)]
        self.t_resid = []
        for nb in range(self.NB):
            c0 = nb * 512
            zts = [self.mm(self._ps8acc(oc), self.zcol128[:, :], self.zeros_b[:, :],
                           comb + [self._resid_war(nb)], start=True, stop=False)
                   for oc in range(KC)]
            last_mm = {}
            for kc2 in range(KC):
                pp = kc2 % 2
                tld = self.dma("sp", self.ctxc[:, pp * 512:(pp + 1) * 512],
                               self.ctxn_d[kc2 * 128:(kc2 + 1) * 128, c0:c0 + 512],
                               comb + [self._ctxc_war[pp]])
                for oc in range(KC):
                    acc = self._ps8acc(oc)
                    last_mm[oc] = self.mm(
                        acc, self.woT[:, kc2 * D + oc * 128: kc2 * D + oc * 128 + 128],
                        self.ctxc[:, pp * 512:(pp + 1) * 512],
                        zts + two + [tld],
                        start=False, stop=(kc2 == KC - 1))
                self._ctxc_war[pp] = last_mm[KC - 1]
            for oc in range(KC):
                t = b.emit("dve", [last_mm[oc]] + self.t_xf,
                           lambda e, oc=oc, c0=c0, acc=self._ps8acc(oc):
                           e.scalar_tensor_tensor(
                               self.x_f[:, oc * S + c0: oc * S + c0 + 512],
                               acc, self.boc[:, oc:oc + 1],
                               self.x_f[:, oc * S + c0: oc * S + c0 + 512],
                               ALU.add, ALU.add))
                self.t_resid.append(t)

    def _ps8acc(self, oc):
        ps = self.psA if oc < 4 else self.psB
        k = oc % 4
        return ps[0:128, k * 512:(k + 1) * 512]

    def _resid_war(self, nb):
        if nb == 0:
            return None
        return self.t_resid[nb * self.KC - 1] if self.t_resid else None

    def layernorm(self, dep):
        b, S, D, KC = self.b, self.S, self.D, self.KC
        sum_toks = []
        half_war = [None, None]
        ci = 0
        for kc in range(KC):
            for nb in range(self.NB):
                c0 = nb * 512
                pp = ci % 2
                h0 = pp * 512
                ci += 1
                tsq = b.emit("act", dep + [half_war[pp]],
                             lambda e, kc=kc, c0=c0, h0=h0: e.activation(
                                 self.x_tm[:, h0:h0 + 512],
                                 self.x_f[:, kc * S + c0: kc * S + c0 + 512],
                                 AF.Square))
                t_s = self.mm(self.psA[0:1, c0:c0 + 512], self.ones_col_f[:, :],
                              self.x_f[:, kc * S + c0: kc * S + c0 + 512],
                              dep, start=(kc == 0), stop=(kc == KC - 1))
                t_q = self.mm(self.psB[0:1, c0:c0 + 512], self.ones_col_f[:, :],
                              self.x_tm[:, h0:h0 + 512], [tsq],
                              start=(kc == 0), stop=(kc == KC - 1))
                half_war[pp] = t_q
                if kc == KC - 1:
                    sum_toks += [t_s, t_q]

        t_r0 = b.emit("dve", sum_toks, lambda e: (
            e.tensor_scalar(self.srow[:, 0:S], self.psA[0:1, 0:S], 1.0 / D, None,
                            ALU.mult),
            e.tensor_scalar(self.rowB[:, 0:S], self.psB[0:1, 0:S], 1.0 / D, None,
                            ALU.mult))[-1])
        t_r1 = b.emit("dve", [t_r0], lambda e: e.tensor_mul(
            self.arow[:, 0:S], self.srow[:, 0:S], self.srow[:, 0:S]))
        t_r = b.emit("dve", [t_r1], lambda e: e.tensor_sub(
            self.rowB[:, 0:S], self.rowB[:, 0:S], self.arow[:, 0:S]))
        t_ln = b.emit("act", [t_r], lambda e: e.activation(
            self.arow[:, 0:S], self.rowB[:, 0:S], AF.Ln, bias=self.eps_col[0:1, :]))
        t_ex = b.emit("act", [t_ln], lambda e: e.activation(
            self.rowB[:, 0:S], self.arow[:, 0:S], AF.Exp, scale=-0.5))
        t_nm = b.emit("dve", [t_ex], lambda e: e.tensor_mul(
            self.arow[:, 0:S], self.srow[:, 0:S], self.rowB[:, 0:S]))
        t_ng = b.emit("dve", [t_nm], lambda e: e.tensor_scalar(
            self.arow[:, 0:S], self.arow[:, 0:S], -1.0, None, ALU.mult))
        bc = []
        for nb in range(self.NB):
            c0 = nb * 512
            bc.append(self.mm(self.psA[0:128, c0:c0 + 512], self.ones_r1f[:, :],
                              self.rowB[:, c0:c0 + 512], [t_ex, sum_toks[-1], t_ng],
                              start=True, stop=True))
            bc.append(self.mm(self.psB[0:128, c0:c0 + 512], self.ones_r1f[:, :],
                              self.arow[:, c0:c0 + 512], [t_ng],
                              start=True, stop=True))
        out_toks = []
        prev = None
        ci = 0
        for kc in range(KC):
            for nb in range(self.NB):
                c0 = nb * 512
                h0 = (ci % 2) * 512
                ci += 1
                sl = slice(kc * S + c0, kc * S + c0 + 512)
                ta = b.emit("dve", bc + [t_r, prev], lambda e, sl=sl, h0=h0, c0=c0:
                            e.tensor_mul(self.x_tm[:, h0:h0 + 512], self.x_f[:, sl],
                                         self.psA[0:128, c0:c0 + 512]))
                prev = b.emit("dve", [ta], lambda e, sl=sl, h0=h0, c0=c0:
                              e.tensor_add(self.x_f[:, sl],
                                           self.x_tm[:, h0:h0 + 512],
                                           self.psB[0:128, c0:c0 + 512]))
                out_toks.append(prev)
        return out_toks

    def quant1(self, dep):
        b, S, KC, NT = self.b, self.S, self.KC, self.NT
        reds = []
        red_t = []
        half_war = [None, None]
        for nb in range(self.NB):
            c0 = nb * 512
            pp = nb % 2
            h0 = pp * 512
            t = b.emit("dve", dep + [half_war[pp]], lambda e, c0=c0, h0=h0:
                       e.scalar_tensor_tensor(
                           self.x_tm[:, h0:h0 + 512], self.x_f[:, c0:c0 + 512],
                           -1.0, self.x_f[:, c0:c0 + 512], ALU.mult, ALU.max))
            for kc in range(1, KC):
                sl = slice(kc * S + c0, kc * S + c0 + 512)
                t = b.emit("dve", [t], lambda e, sl=sl, h0=h0: e.tensor_tensor(
                    self.x_tm[:, h0:h0 + 512], self.x_tm[:, h0:h0 + 512],
                    self.x_f[:, sl], ALU.max))
                t = b.emit("dve", [t], lambda e, sl=sl, h0=h0:
                           e.scalar_tensor_tensor(
                               self.x_tm[:, h0:h0 + 512], self.x_f[:, sl], -1.0,
                               self.x_tm[:, h0:h0 + 512], ALU.mult, ALU.max))
            for j in range(4):
                tt = nb * 4 + j
                bank = self.psA[:, (tt % 4) * 512:(tt % 4) * 512 + 128]
                war = reds[tt - 4] if tt >= 4 else None
                ttr = self.tr(bank, self.x_tm[:, h0 + j * 128: h0 + j * 128 + 128],
                              [t, war], f32=True)
                red_t.append(ttr)
                reds.append(b.emit("dve", [ttr], lambda e, tt=tt, bank=bank:
                                   e.reduce_max(self.stat[:, tt:tt + 1], bank,
                                                axis=mybir.AxisListType.X)))
            half_war[pp] = red_t[-1]

        tsa = b.emit("dve", reds + self.t_wq_done, lambda e: e.tensor_scalar(
            self.stat[:, 16:16 + NT], self.stat[:, 0:NT], 1e-5, None, ALU.max))
        tdv = b.emit("dve", [tsa], lambda e: e.reciprocal(
            self.stat[:, 32:32 + NT], self.stat[:, 16:16 + NT]))
        tdv = b.emit("dve", [tdv], lambda e: e.tensor_scalar(
            self.stat[:, 32:32 + NT], self.stat[:, 32:32 + NT],
            127.0, None, ALU.mult))
        tsb = b.emit("dve", [tdv], lambda e: e.tensor_scalar(
            self.stat[:, 32 + NT:32 + 2 * NT], self.stat[:, 16:16 + NT],
            self.swc[:, 1:2], None, ALU.mult))
        ttr = self.tr(self.psB[0:2 * NT, 0:128], self.stat[:, 32:32 + 2 * NT],
                      [tsb], f32=True)
        tev = b.emit("dve", [ttr], lambda e: e.tensor_copy(
            self.shift32[0:2 * NT, 0:128], self.psB[0:2 * NT, 0:128]))
        td1 = self.dma("gp", self.srow[:, :].rearrange("p (k c) -> p k c", c=128),
                       self.shift32[0:NT, 0:128], [tev], s2s=True)
        td2 = self.dma("gp", self.arow[:, :].rearrange("p (k c) -> p k c", c=128),
                       self.shift32[NT:2 * NT, 0:128],
                       [tev], s2s=True)
        out = []
        ci = 0
        for nb in range(self.NB):
            c0 = nb * 512
            t_sb = self.mm(self.psA[0:128, c0:c0 + 512], self.ones_r1f[:, :],
                           self.srow[:, c0:c0 + 512], [td1, tev, reds[-1]],
                           start=True, stop=True)
            t_ab = self.mm(self.psB[0:128, c0:c0 + 512], self.ones_r1f[:, :],
                           self.arow[:, c0:c0 + 512], [td2, ttr],
                           start=True, stop=True)
            prevq = None
            for kc in range(KC):
                sl = slice(kc * S + c0, kc * S + c0 + 512)
                h0 = (ci % 2) * 512
                ci += 1
                ta = b.emit("dve", [t_sb, prevq], lambda e, sl=sl, c0=c0, h0=h0:
                            e.tensor_mul(self.x_tm[:, h0:h0 + 512], self.x_f[:, sl],
                                         self.psA[0:128, c0:c0 + 512]))
                tb_ = b.emit("dve", [ta], lambda e, h0=h0: e.tensor_scalar(
                    self.x_tm[:, h0:h0 + 512], self.x_tm[:, h0:h0 + 512],
                    RND_C, RND_C, ALU.add, ALU.subtract))
                prevq = b.emit("dve", [tb_, t_ab], lambda e, sl=sl, c0=c0, h0=h0:
                               e.tensor_mul(self.xbf[:, sl],
                                            self.x_tm[:, h0:h0 + 512],
                                            self.psB[0:128, c0:c0 + 512]))
                out.append(prevq)
        return out

    def ffn(self, dep, t_park):
        b, S, D, DH, KC, HC = self.b, self.S, self.D, self.DH, self.KC, self.HC
        RW, NREG = self.RW, self.NREG
        CB_H = 4 * 130
        CB_Q = 5 * 130
        CB_D = 6 * 130
        tw1 = [self.dma("sp", self.w1T[:, kc * DH:(kc + 1) * DH],
                        self.stg_w1t[kc * 128:(kc + 1) * 128, :],
                        self.t_w1t + [t_park] + dep) for kc in range(KC)]
        resid_out = []
        w2last = None
        habs_war = None
        xq2war = [None, None]
        w2cwar = [None, None]
        xq2dve = [None, None]
        for r in range(NREG):
            t0 = r * RW
            e0 = max(0, t0 - 1)
            e1 = min(S, t0 + RW + 1)
            EW = e1 - e0
            o0 = t0 - e0
            txl = self.dma("sp",
                           self.x_tm[:, 0:D].rearrange("p (k c) -> p k c", c=RW),
                           bass.AP(self.x_d, t0, [[KC * S, 128], [S, KC], [1, RW]]),
                           [t_park, w2last] + dep)
            conv_done = {}
            habs = None
            for hc in range(HC):
                slot = self.psA[:, (hc % 4) * 512:(hc % 4) * 512 + EW]
                last = None
                for kc in range(KC):
                    last = self.mm(slot,
                                   self.w1T[:, kc * DH + hc * 128:
                                            kc * DH + hc * 128 + 128],
                                   self.xbf[:, kc * S + e0: kc * S + e1],
                                   tw1 + dep + [conv_done.get(hc - 4), w2last],
                                   start=(kc == 0), stop=(kc == KC - 1))
                c1s = (hc % 2) * 2 * 130
                c2s = c1s + 130
                if o0 == 0:
                    ta = b.emit("dve", [last, conv_done.get(hc - 2)],
                                lambda e, slot=slot, hc=hc, c1s=c1s: (
                        e.tensor_scalar(self.cbuf[:, c1s:c1s + 1],
                                        self.cbc[:, hc:hc + 1], 1.0, None, ALU.mult),
                        e.tensor_scalar(self.cbuf[:, c1s + 1:c1s + RW],
                                        slot[:, 0:RW - 1],
                                        self.cwc[:, hc * 3:hc * 3 + 1],
                                        self.cbc[:, hc:hc + 1],
                                        ALU.mult, ALU.add))[-1])
                else:
                    ta = b.emit("dve", [last, conv_done.get(hc - 2)],
                                lambda e, slot=slot, hc=hc, c1s=c1s, o0=o0:
                                e.tensor_scalar(self.cbuf[:, c1s:c1s + RW],
                                                slot[:, o0 - 1:o0 - 1 + RW],
                                                self.cwc[:, hc * 3:hc * 3 + 1],
                                                self.cbc[:, hc:hc + 1],
                                                ALU.mult, ALU.add))
                tb_ = b.emit("dve", [ta], lambda e, slot=slot, hc=hc, c1s=c1s,
                             c2s=c2s, o0=o0: e.scalar_tensor_tensor(
                    self.cbuf[:, c2s:c2s + RW], slot[:, o0:o0 + RW],
                    self.cwc[:, hc * 3 + 1:hc * 3 + 2], self.cbuf[:, c1s:c1s + RW],
                    ALU.mult, ALU.add))
                if e1 == S:
                    tc_ = b.emit("dve", [tb_], lambda e, slot=slot, hc=hc, c1s=c1s,
                                 c2s=c2s, o0=o0: (
                        e.scalar_tensor_tensor(
                            self.cbuf[:, c1s:c1s + RW - 1],
                            slot[:, o0 + 1:o0 + RW],
                            self.cwc[:, hc * 3 + 2:hc * 3 + 3],
                            self.cbuf[:, c2s:c2s + RW - 1], ALU.mult, ALU.add),
                        e.tensor_copy(self.cbuf[:, c1s + RW - 1:c1s + RW],
                                      self.cbuf[:, c2s + RW - 1:c2s + RW]))[-1])
                else:
                    tc_ = b.emit("dve", [tb_], lambda e, slot=slot, hc=hc, c1s=c1s,
                                 c2s=c2s, o0=o0: e.scalar_tensor_tensor(
                        self.cbuf[:, c1s:c1s + RW], slot[:, o0 + 1:o0 + 1 + RW],
                        self.cwc[:, hc * 3 + 2:hc * 3 + 3],
                        self.cbuf[:, c2s:c2s + RW], ALU.mult, ALU.add))
                conv_done[hc] = tc_
                tsil = b.emit("act", [tc_], lambda e, hc=hc, c1s=c1s: e.activation(
                    self.h2reg[:, hc * RW:(hc + 1) * RW], self.cbuf[:, c1s:c1s + RW],
                    AF.Silu))
                if hc == 0:
                    habs = b.emit("dve", [tsil, habs_war], lambda e, hc=hc:
                                  e.scalar_tensor_tensor(
                        self.cbuf[:, CB_H:CB_H + RW],
                        self.h2reg[:, hc * RW:(hc + 1) * RW], -1.0,
                        self.h2reg[:, hc * RW:(hc + 1) * RW], ALU.mult, ALU.max))
                else:
                    habs = b.emit("dve", [tsil, habs], lambda e, hc=hc:
                                  e.tensor_tensor(
                        self.cbuf[:, CB_H:CB_H + RW], self.cbuf[:, CB_H:CB_H + RW],
                        self.h2reg[:, hc * RW:(hc + 1) * RW], ALU.max))
                    habs = b.emit("dve", [habs], lambda e, hc=hc:
                                  e.scalar_tensor_tensor(
                        self.cbuf[:, CB_H:CB_H + RW],
                        self.h2reg[:, hc * RW:(hc + 1) * RW],
                        -1.0, self.cbuf[:, CB_H:CB_H + RW], ALU.mult, ALU.max))
            bank = self.psA[0:128, 0:128]
            ttr = self.tr(bank, self.cbuf[:, CB_H:CB_H + RW],
                          [habs, conv_done[HC - 1]], f32=True)
            habs_war = ttr
            red = b.emit("dve", [ttr], lambda e, bank=bank:
                         e.reduce_max(self.stat[:, 0:1], bank,
                                      axis=mybir.AxisListType.X))
            tsa = b.emit("dve", [red], lambda e: e.tensor_scalar(
                self.stat[:, 4:5], self.stat[:, 0:1], 1e-5, None, ALU.max))
            tdv = b.emit("dve", [tsa], lambda e: e.reciprocal(
                self.stat[:, 8:9], self.stat[:, 4:5]))
            tdv = b.emit("dve", [tdv], lambda e: e.tensor_scalar(
                self.stat[:, 8:9], self.stat[:, 8:9], 127.0, None, ALU.mult))
            tsb = b.emit("dve", [tdv], lambda e: e.tensor_scalar(
                self.stat[:, 9:10], self.stat[:, 4:5],
                self.swc[:, 3:4], None, ALU.mult))
            ttr2 = self.tr(self.psB[0:2, 0:128], self.stat[:, 8:10],
                           [tsb, self._ctx_free], f32=True)
            tev = b.emit("dve", [ttr2], lambda e: e.tensor_copy(
                self.shift32[0:2, 0:128], self.psB[0:2, 0:128]))
            td1 = self.dma("gp", self.srow[:, 0:128], self.shift32[0:1, 0:128],
                           [tev], s2s=True)
            td2 = self.dma("gp", self.arow[:, 0:128], self.shift32[1:2, 0:128],
                           [tev], s2s=True)
            t_sb = self.mm(self.psA[0:128, 0:RW], self.ones_r1f[:, :],
                           self.srow[:, 0:RW], [td1, red], start=True, stop=True)
            t_ab = self.mm(self.psA[0:128, 512:512 + RW], self.ones_r1f[:, :],
                           self.arow[:, 0:RW], [td2, red], start=True, stop=True)
            ta2 = b.emit("dve", [t_ab], lambda e: e.tensor_copy(
                self.a2sb[:, 0:RW], self.psA[0:128, 512:512 + RW]))
            zts = [self.mm(self.psB[0:128, bank * 512:(bank + 1) * 512],
                           self.zcol128[:, :], self.zeros_b[:, :],
                           [tev, w2cwar[0], w2cwar[1]], start=True, stop=False)
                   for bank in range((KC + 1) // 2)]
            for hc in range(HC):
                pp = hc % 2
                xs = pp * 128
                tq1 = b.emit("dve", [t_sb, xq2war[pp], ta2,
                                     xq2dve[0], xq2dve[1]],
                             lambda e, hc=hc: e.tensor_mul(
                                 self.cbuf[:, CB_Q:CB_Q + RW],
                                 self.h2reg[:, hc * RW:(hc + 1) * RW],
                                 self.psA[0:128, 0:RW]))
                tq2 = b.emit("dve", [tq1, xq2war[pp]],
                             lambda e, xs=xs: e.tensor_scalar(
                                 self.xq2c[:, xs:xs + RW],
                                 self.cbuf[:, CB_Q:CB_Q + RW], RND_C, RND_C,
                                 ALU.add, ALU.subtract))
                tl = self.dma("sp", self.w2Tc[:, pp * D:(pp + 1) * D],
                              self.stg_w2t[hc * 128:(hc + 1) * 128, :],
                              self.t_w2t + [w2cwar[pp]])
                for oc in range(KC):
                    slot = self.psB[0:128, (oc // 2) * 512 + (oc % 2) * 256:
                                    (oc // 2) * 512 + (oc % 2) * 256 + RW]
                    tmm = self.mm(slot,
                                  self.w2Tc[:, pp * D + oc * 128:
                                            pp * D + oc * 128 + 128],
                                  self.xq2c[:, xs:xs + RW], zts + [tq2, tl],
                                  start=False,
                                  stop=(hc == HC - 1 and
                                        (oc % 2 == 1 or oc == KC - 1)))
                w2cwar[pp] = tmm
                xq2war[pp] = tmm
                xq2dve[pp] = tq2
            tprev = txl
            for oc in range(KC):
                slot = self.psB[0:128, (oc // 2) * 512 + (oc % 2) * 256:
                                (oc // 2) * 512 + (oc % 2) * 256 + RW]
                ta = b.emit("dve", [w2cwar[0], w2cwar[1], ta2,
                                    xq2dve[0], xq2dve[1], tprev],
                            lambda e, slot=slot: e.tensor_mul(
                                self.cbuf[:, CB_D:CB_D + RW], slot,
                                self.a2sb[:, 0:RW]))
                tprev = b.emit("dve", [ta, txl], lambda e, oc=oc: e.tensor_add(
                    self.x_tm[:, oc * RW:(oc + 1) * RW],
                    self.cbuf[:, CB_D:CB_D + RW],
                    self.x_tm[:, oc * RW:(oc + 1) * RW]))
            tst = self.dma("sp",
                           bass.AP(self.x_d, t0, [[KC * S, 128], [S, KC], [1, RW]]),
                           self.x_tm[:, 0:D].rearrange("p (k c) -> p k c", c=RW),
                           [tprev])
            resid_out.append(tst)
            w2last = tst
        trl = self.dma("sp", self.x_f[:, :], self.x_d[:, :], resid_out)
        return [trl]

    def output(self, dep):
        b, S, KC, NT = self.b, self.S, self.KC, self.NT
        prev_store = None
        for tt in range(NT):
            evs = []
            for kc in range(KC):
                ps = self.psA if kc < 4 else self.psB
                bank = ps[:, (kc % 4) * 512:(kc % 4) * 512 + 128]
                ttr = self.tr(bank,
                              self.x_f[:, kc * S + tt * 128: kc * S + tt * 128 + 128],
                              dep + [prev_store], start=True,
                              stop=True, f32=True)
                evs.append(b.emit("dve", [ttr], lambda e, kc=kc, bank=bank:
                                  e.tensor_copy(self.x_tm[:, kc * 128:(kc + 1) * 128],
                                                bank)))
            prev_store = self.dma("sp", self.out_d[tt * 128:(tt + 1) * 128, :],
                                  self.x_tm[:, 0:self.D], evs)
        self.t_out = prev_store

    def build_all(self, stage=9):
        self.consts()
        self.small_inputs()
        self.load_x()
        dep = self.t_x + self.t_small
        if stage >= 2:
            self.weights_qkvo()
            self.weights_w12()
            dep = (dep + self.t_wq_done + self.t_w2t + self.t_w1t +
                   [t for v in self.t_wT.values() for t in v])
        if stage >= 3:
            self.attention()
            dep = self.t_ctx_dma + self.t_sums_dma
        if stage >= 4:
            self.post_attention()
            self._ctx_free = self.t_resid[-1]
            tp1 = self.tap("xattn", self.x_f[:, :], self.t_resid)
            tp2 = self.tap("ctxn",
                           bass.AP(self.ctxn_d, 0,
                                   [[self.S, 128], [128 * self.S, self.NP],
                                    [1, self.S]]),
                           self.t_comb)
            dep = self.t_resid + [tp1, tp2]
        if stage >= 5:
            t_ln1 = self.layernorm(dep)
            tp3 = self.tap("x1", self.x_f[:, :], t_ln1)
            dep = t_ln1 + [tp3]
        if stage >= 6:
            tq1 = self.quant1(dep)
            tp4 = self.tap("xq1", self.xbf[:, :], tq1)
            dep = tq1 + [tp4]
        if stage >= 7:
            t_park = self.dma("sp", self.x_d[:, :], self.x_f[:, :], dep)
            dep = self.ffn(dep, t_park)
        if stage >= 8:
            dep = self.layernorm(dep)
        self.output(dep)
        fin = [self.t_out] + self.b.dma_totals("sp") + self.b.dma_totals("gp")
        self.b.emit("dve", fin, lambda e: None, inc=0)
        nc = self.nc
        from contextlib import ExitStack
        names = ["pe", "act", "dve", "gpc"] + \
            [f"sp{r}" for r in range(Builder.NRING)] + \
            [f"gp{r}" for r in range(Builder.NRING)]
        with ExitStack() as ctx:
            sems = {n: ctx.enter_context(nc.semaphore(f"s_{n}")) for n in names}
            block = ctx.enter_context(nc.Block())
            self.b.build(block, sems)
        return nc


_PROG_CACHE = {}


def get_nc(S, D, H, DH):
    key = (S, D, H, DH)
    if key not in _PROG_CACHE:
        _PROG_CACHE[key] = Prog(S, D, H, DH).build_all()
    return _PROG_CACHE[key]


def kernel(L0_attn_x, L0_attn_Wq, L0_attn_Wk, L0_attn_Wv, L0_attn_Wo, L0_attn_b_o,
           L0_attn_rel_k_emb, L0_attn_rel_v_emb, L0_ffn_w1, L0_ffn_conv_w,
           L0_ffn_conv_b, L0_ffn_w2):
    x = np.asarray(L0_attn_x)
    B, S, D = x.shape
    DH = np.asarray(L0_ffn_w1).shape[0]
    nc = get_nc(S, D, 16, DH)
    f = np.float32
    shared = {
        "wq_in": np.ascontiguousarray(L0_attn_Wq, f),
        "wk_in": np.ascontiguousarray(L0_attn_Wk, f),
        "wv_in": np.ascontiguousarray(L0_attn_Wv, f),
        "wo_in": np.ascontiguousarray(L0_attn_Wo, f),
        "bo_in": np.ascontiguousarray(L0_attn_b_o, f).reshape(1, D),
        "rk_in": np.ascontiguousarray(L0_attn_rel_k_emb, f),
        "rv_in": np.ascontiguousarray(L0_attn_rel_v_emb, f),
        "w1_in": np.ascontiguousarray(L0_ffn_w1, f),
        "cw_in": np.ascontiguousarray(L0_ffn_conv_w, f).reshape(DH, 3),
        "cb_in": np.ascontiguousarray(L0_ffn_conv_b, f).reshape(1, DH),
        "w2_in": np.ascontiguousarray(L0_ffn_w2, f),
    }
    in_maps = []
    for c in range(B):
        m = dict(shared)
        m["x_in"] = np.ascontiguousarray(x[c], f)
        in_maps.append(m)
    res = run_bass_kernel_spmd(nc, in_maps, core_ids=list(range(B)))
    return np.stack([np.asarray(r["out"], dtype=np.float32) for r in res.results])
